# version 26
# baseline (speedup 1.0000x reference)
import contextlib
import numpy as np
import concourse.bass as bass
import concourse.mybir as mybir
from concourse.bass_utils import run_bass_kernel_spmd

F32 = mybir.dt.float32
BF16 = mybir.dt.bfloat16
AF = mybir.ActivationFunctionType
ALU = mybir.AluOpType
AX = mybir.AxisListType

NCORES = 8
D = 1024
T = 16384
Tc = 2048
P = 128
NP = 8
L = 128
NCH = 16
TCH = 512
NST = 4
H = 16
NH = 64
EPS = 1e-5
CLAMP = 75.0

ENGINES = ("sync", "tensor", "vector", "scalar", "gpsimd")
SEMS = ("mm", "vv", "ss", "cc", "gg")
V, A_, G, SY, TE = "vector", "scalar", "gpsimd", "sync", "tensor"


class Sched:
    def __init__(self):
        self.q = {e: [] for e in ENGINES}
        self.emitted = {s: 0 for s in SEMS}
        self.handles = {}
        self.bg = []
        self.bg_hi = []
        self.bar_snap = {}
        self.barscr = None

    def sem(self, name):
        if name not in self.emitted:
            self.emitted[name] = 0
        return name

    def wait(self, eng, sem, val):
        assert val <= self.emitted[sem], (eng, sem, val, self.emitted[sem])
        if val <= 0:
            return
        self.q[eng].append(lambda e, s=sem, v=val: e.wait_ge(self.handles[s], v))

    def op(self, eng, fn, inc=None, n=1):
        if inc is not None:
            self.emitted[inc] += n
        def run(e, fn=fn, inc=inc, n=n):
            r = fn(e)
            if inc is not None:
                r.then_inc(self.handles[inc], n)
        self.q[eng].append(run)
        return self.emitted[inc] if inc is not None else None

    def drain(self, eng):
        self.q[eng].append(lambda e: e.drain())

    def pump_bg(self, n=1):
        for i in range(n):
            if self.bg_hi and i == 0:
                self.bg_hi.pop(0)()
            elif self.bg:
                self.bg.pop(0)()
            elif self.bg_hi:
                self.bg_hi.pop(0)()

    def dma(self, eng, out_ap, in_ap, stream="Lmisc", slow=False, accum=False):
        sem = self.sem(stream)
        if accum:
            return self.op(eng, lambda e, o=out_ap, i=in_ap: e.dma_start(
                out=o, in_=i, accum_op=ALU.add), inc=sem, n=16)
        if slow:
            return self.op(eng, lambda e, o=out_ap, i=in_ap: e.dma_start(
                out=o, in_=i, allow_slow_non_contiguous=True), inc=sem, n=16)
        return self.op(eng, lambda e, o=out_ap, i=in_ap: e.dma_start(out=o, in_=i),
                       inc=sem, n=16)

    def barrier(self, skip=("cc",)):
        snap = dict(self.emitted)
        for sem, v in snap.items():
            if sem in skip or v <= 0 or sem == "vv":
                continue
            if self.bar_snap.get(sem, 0) == v:
                continue
            self.wait(V, sem, v)
        self.op(V, lambda e: e.memset(self.barscr[:, :], 0.0), inc="vv")
        t = self.emitted["vv"]
        self.bar_snap = dict(self.emitted)
        for eng in ENGINES:
            if eng == V:
                continue
            self.wait(eng, "vv", t)


def build_program(stop=None):
    nc = bass.Bass("TRN2", target_bir_lowering=False)
    S = Sched()
    def _stopped(name):
        return stop is not None and name == stop

    x_in = nc.dram_tensor("x", [Tc + 1, D], F32, kind="ExternalInput").ap()
    wts = {}
    for wn in ["ww", "wk", "wv", "wr", "wg", "fr", "fk", "fv", "wo"]:
        wts[wn] = nc.dram_tensor("w_" + wn, [D, D], F32, kind="ExternalInput").ap()
    vecs = nc.dram_tensor("vecs", [10, D], F32, kind="ExternalInput").ap()
    ident_in = nc.dram_tensor("ident", [P, P], F32, kind="ExternalInput").ap()
    maska_in = nc.dram_tensor("maska", [P, 4 * P], F32, kind="ExternalInput").ap()
    onesb_in = nc.dram_tensor("onesb", [P, 2], F32, kind="ExternalInput").ap()
    cmask_in = nc.dram_tensor("cmask", [P, 8], F32, kind="ExternalInput").ap()
    pmask_in = nc.dram_tensor("pmask", [P, 1], F32, kind="ExternalInput").ap()
    eps_in = nc.dram_tensor("epsc", [P, 1], F32, kind="ExternalInput").ap()
    out_hbm = nc.dram_tensor("out", [Tc, D], F32, kind="ExternalOutput").ap()

    s_xnT = nc.dram_tensor("s_xnT", [16, P, D], F32)
    s_row = nc.dram_tensor("s_row", [2, D], F32)
    spill = {}
    for n in ["pexp", "k", "r", "g", "rt", "khat", "rf", "ffn"]:
        spill[n] = nc.dram_tensor("s_" + n, [NP, P, Tc], F32)
    s_vt = nc.dram_tensor("s_vt", [NCH, P, D], F32)
    s_sloc = nc.dram_tensor("s_sloc", [NCH, P, D], F32)
    cc_in = nc.dram_tensor("cc_in", [P, D + NP], F32)
    cc_out = nc.dram_tensor("cc_out", [NCORES, P, D + NP], F32)

    ctx = contextlib.ExitStack()
    sb = lambda name, shape: ctx.enter_context(nc.sbuf_tensor(name, shape, F32))
    sbB = lambda name, shape: ctx.enter_context(nc.sbuf_tensor(name, shape, BF16))

    wbh = [sbB("wbh0", [P, NP * D]), sbB("wbh1", [P, NP * D])]
    wbl = [sbB("wbl0", [P, NP * D]), sbB("wbl1", [P, NP * D])]
    wstg = sb("wstg", [P, D])
    mxh = [sbB("mxh0", [P, 4096]), sbB("mxh1", [P, 4096])]
    mxl = [sbB("mxl0", [P, 4096]), sbB("mxl1", [P, 4096])]
    bigA = sb("bigA", [P, NP * (Tc + 1)])
    bigC = sb("bigC", [P, 6656])
    xtok = [bigC[:, 2048:3072], bigC[:, 3072:4096]]
    xn_tok = bigC[:, 4096:5120]
    xn2_tok = bigC[:, 5120:6144]
    s_pp = [sb("spp0", [P, D]), sb("spp1", [P, D])]
    s_inb = sb("s_inb", [P, D])
    dvals_sb = sb("dvals", [P, NCH * H])
    cend_sb = sb("cend", [P, NP * NCH])
    ident = sb("identt", [P, P])
    maska = sbB("maskaa", [P, 4 * P])
    onesb = sb("onesbb", [P, 2])
    cmask = sb("cmaskk", [P, 8])
    pmask = sb("pmaskk", [P, 1])
    epsc = sb("epscc", [P, 1])
    cols = sb("colsb", [P, 160])
    rawmix = dvals_sb[:, 0:4 * NP]

    dtilc = None
    dfc = sb("dfc", [P, NCH])
    statc = dfc
    S.barscr = dfc[0:1, 0:1]
    dtilc = dfc[:, 0:8]

    lmaskc = dfc[:, 8:16]
    plocc = lmaskc

    p_proj = [ctx.enter_context(nc.psum_tensor("pproj0", [P, 512], F32)),
              ctx.enter_context(nc.psum_tensor("pproj1", [P, 512], F32))]
    p_a = [ctx.enter_context(nc.psum_tensor("pa0", [P, 512], F32)),
           ctx.enter_context(nc.psum_tensor("pa1", [P, 512], F32))]
    p_yt = ctx.enter_context(nc.psum_tensor("pyt", [P, 1024], F32))
    p_m = [ctx.enter_context(nc.psum_tensor("pm0", [P, 512], F32)),
           ctx.enter_context(nc.psum_tensor("pm1", [P, 512], F32))]

    for _s in ["Lc", "Lx0", "Lx1", "Gstg", "Gmisc", "Lw0", "Lw1", "Gsp",
               "Lp", "Lk2", "Lr2", "Grt", "Gkh", "Lvt0", "Lvt1", "Gsl", "Gcc",
               "Lc2", "Gffn", "Lrf0", "Lrf1", "Lgb0", "Lgb1", "Lg", "Lkh",
               "Lrt", "Lvtb", "Lslb", "Lfn0", "Lfn1", "Lxtok", "Gout"]:
        S.sem(_s)

    t_ident = S.dma(SY, ident[:, :], ident_in, stream="Lc")
    S.dma(G, maska[:, :], maska_in, stream="Lmk")
    S.dma(SY, onesb[:, :], onesb_in, stream="Lc")
    S.dma(SY, cmask[:, :], cmask_in, stream="Lc")
    S.dma(SY, pmask[:, :], pmask_in, stream="Lc")
    S.dma(SY, epsc[:, :], eps_in, stream="Lc")
    for i in range(4):
        S.dma(SY, rawmix[:, i * NP:(i + 1) * NP],
              vecs[i].rearrange("(p j) -> j p", j=P), stream="Lc", slow=True)
    for i, co in [(4, 64), (5, 72), (6, 80), (7, 88), (8, 96), (9, 104)]:
        S.dma(SY, cols[:, co:co + NP], vecs[i].rearrange("(p j) -> j p", j=P), stream="Lc", slow=True)
    t_consts = S.emitted["Lc"]
    S.wait(A_, "Lc", t_consts)
    S.op(A_, lambda e: e.activation(cols[:, 0:32], rawmix[:, 0:32], AF.Sigmoid), inc="ss")
    t_sig = S.emitted["ss"]
    S.wait(V, "ss", t_sig)
    S.wait(V, "Lc", t_consts)
    S.op(V, lambda e: e.tensor_scalar(cols[:, 32:64], cols[:, 0:32], -1.0, 1.0,
                                      ALU.mult, ALU.add), inc="vv")
    S.op(V, lambda e: e.tensor_scalar(cols[:, 128:144], cols[:, 96:112], -1.0, 1.0,
                                      ALU.mult, ALU.add), inc="vv")
    S.op(V, lambda e: e.memset(cols[:, 144:160], 0.0), inc="vv")
    S.barrier()

    if _stopped("P0"):
        return _finish(nc, S, ctx)

    t1scr = bigC[:, 0:512]
    mixscr = [bigC[:, 512:1024], bigC[:, 1024:1536]]
    mixscr2 = [s_inb[:, 0:512], s_inb[:, 512:1024]]
    scrE = bigC[:, 1536:3584]
    vscr = bigC[:, 3584:4096]
    relscr = bigC[:, 4096:4608]
    sqf = bigC[:, 4608:5120]
    fb = [bigC[:, 5120:5632], bigC[:, 5632:6144]]
    vtb = [s_pp[1][:, 0:512], s_pp[1][:, 512:1024]]
    rfstr = [vscr, bigC[:, 6144:6656]]
    mx_used = [0, 0]
    w_ready = {}
    w_free_t = [0, 0]
    w_slot = {}
    wstg_last = [(0, 0), (0, 0)]
    _order = ["ww", "wk", "wv", "wr", "wg", "fr", "fk", "fv", "wo"]
    S.sem("Lws0")
    S.sem("Lws1")
    S.sem("Lwh0")
    S.sem("Lwh1")

    def load_weight(i):
        wn = _order[i]
        slot = i % 2
        w_slot[wn] = slot
        w_ready[wn] = [None, None]
        free_mm = w_free_t[slot]

        def mk(q):
            def c():
                if q == 0:
                    S.wait(SY, "mm", free_mm)
                ws, wv = wstg_last[q % 2]
                S.wait(SY, "ss", ws)
                S.wait(SY, "vv", wv)
                src = wts[wn].rearrange("(kk j) d -> j kk d", j=P)[
                    :, q // 2, 512 * (q % 2):512 * (q % 2) + 512]
                half = wstg[:, 512 * (q % 2):512 * (q % 2) + 512]
                t_l = S.dma(SY, half, src, stream="Lws%d" % (q % 2))
                S.wait(A_, "Lws%d" % (q % 2), t_l)
                if q == 0:
                    S.wait(A_, "mm", free_mm)
                S.op(A_, lambda e: e.copy(
                    wbh[slot][:, 512 * q:512 * q + 512], half), inc="ss")
                th = S.emitted["ss"]
                S.wait(V, "ss", th)
                if q == 0:
                    S.wait(V, "mm", free_mm)
                S.op(V, lambda e: e.tensor_tensor(
                    wbl[slot][:, 512 * q:512 * q + 512], half,
                    wbh[slot][:, 512 * q:512 * q + 512], ALU.subtract), inc="vv")
                wstg_last[q % 2] = (th, S.emitted["vv"])
                if q == 15:
                    w_ready[wn][0] = th
                    w_ready[wn][1] = S.emitted["vv"]
            return c
        for q in range(16):
            S.bg.append(mk(q))

    wi_next = [0]
    wi_pending = [2]

    def finish_weight(wn):
        w_free_t[w_slot[wn]] = S.emitted["mm"]
        wi_pending[0] += 1

    def pump_weights():
        while wi_pending[0] > 0 and wi_next[0] < len(_order):
            load_weight(wi_next[0])
            wi_next[0] += 1
            wi_pending[0] -= 1
    pump_weights()

    mixscr_last = [[(0, 0), (0, 0)], [(0, 0), (0, 0)]]
    mixscr_sets = [mixscr, mixscr2]

    def emit_mix_bg(st, sig_off, oms_off, src_fn, buf, guard_mm):
        holder = [None]
        t1p = [t1scr, relscr]
        ms = mixscr_sets[buf]

        def mk(pp):
            def c():
                if pp == 0:
                    S.wait(V, "mm", guard_mm[0])
                    S.wait(A_, "mm", guard_mm[0])
                for j in (0, 1):
                    p = pp + j
                    cur = src_fn(p)[:, 1 + TCH * st: 1 + TCH * st + TCH]
                    ps, pv = mixscr_last[buf][j]
                    S.wait(V, "ss", ps)
                    S.wait(V, "vv", pv)
                    S.op(V, lambda e, cur=cur, p=p, j=j, so=sig_off: e.tensor_scalar(
                        t1p[j][:, :], cur, cols[:, so + p:so + p + 1], None,
                        ALU.mult, ALU.bypass), inc="vv")
                S.drain(V)
                for j in (0, 1):
                    p = pp + j
                    shf = src_fn(p)[:, TCH * st: TCH * st + TCH]
                    S.op(V, lambda e, shf=shf, p=p, j=j, oo=oms_off: e.scalar_tensor_tensor(
                        ms[j], shf, cols[:, oo + p:oo + p + 1],
                        t1p[j][:, :], ALU.mult, ALU.add), inc="vv")
                S.drain(V)
                t_m = S.emitted["vv"]
                S.wait(A_, "vv", t_m)
                for j in (0, 1):
                    p = pp + j
                    S.op(A_, lambda e, p=p, j=j, buf=buf: e.copy(
                        mxh[buf][:, TCH * p:TCH * p + TCH], ms[j]), inc="ss")
                t_h = S.emitted["ss"]
                S.wait(V, "ss", t_h)
                for j in (0, 1):
                    p = pp + j
                    S.op(V, lambda e, p=p, j=j, buf=buf: e.tensor_tensor(
                        mxl[buf][:, TCH * p:TCH * p + TCH], ms[j],
                        mxh[buf][:, TCH * p:TCH * p + TCH], ALU.subtract), inc="vv")
                S.drain(V)
                for j in (0, 1):
                    mixscr_last[buf][j] = (t_h, S.emitted["vv"])
                if pp == NP - 2:
                    holder[0] = (S.emitted["ss"], S.emitted["vv"])
            return c
        for pp in range(0, NP, 2):
            S.bg_hi.append(mk(pp))
        return holder

    def emit_mix(st, sig_off, oms_off, src_fn):
        buf = st % 2
        h = emit_mix_bg(st, sig_off, oms_off, src_fn,
                        buf, [max(mx_used[buf], 0)])
        while h[0] is None:
            S.pump_bg(1)
        return h[0]

    proj_bank_free = [("vv", 0), ("vv", 0)]

    def emit_matmuls(wn, rhs_hi, rhs_lo, t_rhs, evac_cb):
        if wn not in w_slot:
            pump_weights()
        while w_ready[wn][1] is None:
            S.pump_bg(4)
        if isinstance(t_rhs, list):
            while t_rhs[0] is None:
                S.pump_bg(1)
            t_rhs = t_rhs[0]
        slot = w_slot[wn]
        S.pump_bg(1)
        S.pump_bg(1)
        S.wait(TE, "ss", max(w_ready[wn][0], t_rhs[0]))
        S.wait(TE, "vv", max(w_ready[wn][1], t_rhs[1]))
        for dd in range(NP):
            bank = dd % 2
            sem, val = proj_bank_free[bank]
            S.wait(TE, sem, val)
            for kk in range(NP):
                wsl = slice(kk * D + dd * P, kk * D + dd * P + P)
                rsl = slice(TCH * kk, TCH * kk + TCH)
                S.op(TE, lambda e, b=bank, slot=slot, wsl=wsl, rsl=rsl, kk=kk:
                     e.matmul(p_proj[b][:, :], wbh[slot][:, wsl], rhs_hi[:, rsl],
                              start=(kk == 0), stop=False), inc="mm")
                S.op(TE, lambda e, b=bank, slot=slot, wsl=wsl, rsl=rsl:
                     e.matmul(p_proj[b][:, :], wbh[slot][:, wsl], rhs_lo[:, rsl],
                              start=False, stop=False), inc="mm")
                S.op(TE, lambda e, b=bank, slot=slot, wsl=wsl, rsl=rsl, kk=kk:
                     e.matmul(p_proj[b][:, :], wbl[slot][:, wsl], rhs_hi[:, rsl],
                              start=False, stop=(kk == NP - 1)), inc="mm")
            evac_cb(dd, bank, S.emitted["mm"])
            S.pump_bg(4)

    def set_bank_free(bank, sem, val):
        proj_bank_free[bank] = (sem, val)

    scrE_t = [0]

    def _set_scrE(v):
        scrE_t[0] = v

    def spill_batch(dram, st, dd):
        dst = dram.ap()[dd - 3:dd + 1].rearrange(
            "p i t -> i p t")[:, :, TCH * st:TCH * st + TCH]
        return S.dma(G, dst, scrE.rearrange("i (p t) -> i p t", t=TCH),
                     stream="Gsp")

    xn2v = lambda p: bigA[:, p * (Tc + 1):(p + 1) * (Tc + 1)]
    stg = bigC[:, 0:1024]
    sqscr = bigC[:, 1024:2048]
    pm_ev = [("vv", 0), ("vv", 0)]
    t_stg_free = 0
    t_xt_used = [("vv", 0), ("vv", 0)]
    t_tile_trans = 0
    t_apply0 = 0
    for j in range(17):
        S.pump_bg(2)
        w = 128 if j < 16 else 1
        xt = xtok[j % 2]
        sem, val = t_xt_used[j % 2]
        S.wait(SY, sem, val)
        t_ld = S.dma(SY, xt[0:w, :], x_in[128 * j:128 * j + w, :],
                     stream="Lx%d" % (j % 2))
        for ln in range(2):
            src = xt[0:w, :] if ln == 0 else xn_tok[0:w, :]
            o = 7 * ln
            if ln == 0:
                S.wait(V, "Lx%d" % (j % 2), t_ld)
                S.wait(A_, "Lx%d" % (j % 2), t_ld)
            else:
                S.wait(A_, "vv", t_apply0)
            S.op(V, lambda e, s=src, o=o, w=w: e.tensor_reduce(
                statc[0:w, o:o + 1], s, AX.X, ALU.add), inc="vv")
            S.drain(V)
            S.op(A_, lambda e, s=src, o=o, w=w: e.activation(
                sqscr[0:w, :], s, AF.Square, accum_out=statc[0:w, o + 1:o + 2]), inc="ss")
            t_acc = S.emitted["ss"]
            S.op(V, lambda e, o=o, w=w: e.tensor_scalar(
                statc[0:w, o + 2:o + 3], statc[0:w, o:o + 1], 1.0 / D, None,
                ALU.mult, ALU.bypass), inc="vv")
            S.drain(V)
            S.op(V, lambda e, o=o, w=w: e.tensor_tensor(
                statc[0:w, o + 3:o + 4], statc[0:w, o + 2:o + 3],
                statc[0:w, o + 2:o + 3], ALU.mult), inc="vv")
            S.drain(V)
            S.op(V, lambda e, o=o, w=w: e.tensor_scalar(
                statc[0:w, o + 4:o + 5], statc[0:w, o + 3:o + 4], -1.0, EPS,
                ALU.mult, ALU.add), inc="vv")
            t_pre = S.emitted["vv"]
            S.wait(A_, "vv", t_pre)
            S.drain(A_)
            S.op(A_, lambda e, o=o, w=w: e.activation(
                statc[0:w, o + 5:o + 6], statc[0:w, o + 1:o + 2], AF.Sqrt,
                bias=statc[0:w, o + 4:o + 5], scale=1.0 / D), inc="ss")
            t_std = S.emitted["ss"]
            S.wait(V, "ss", t_std)
            S.op(V, lambda e, o=o, w=w: e.reciprocal(
                statc[0:w, o + 6:o + 7], statc[0:w, o + 5:o + 6]), inc="vv")
            S.drain(V)
            dst = xn_tok if ln == 0 else xn2_tok
            S.wait(V, "mm", t_tile_trans)
            S.op(V, lambda e, s=src, d=dst, o=o, w=w: e.tensor_scalar(
                d[0:w, :], s, statc[0:w, o + 2:o + 3], statc[0:w, o + 6:o + 7],
                ALU.subtract, ALU.mult), inc="vv")
            if ln == 0:
                t_apply0 = S.emitted["vv"]
                S.drain(V)
                t_xt_used[j % 2] = ("vv", S.emitted["vv"])
        t_apply = S.emitted["vv"]
        if j < 16:
            S.wait(TE, "vv", t_apply)
            S.wait(TE, "Lc", t_consts)
            for half in range(2):
                for src_i, (srcT, bank) in enumerate([(xn_tok, 0), (xn2_tok, 1)]):
                    sem, val = pm_ev[bank]
                    S.wait(TE, sem, val)
                    for q in range(4):
                        pp_ = 4 * half + q
                        S.op(TE, lambda e, s=srcT, q=q, pp_=pp_, b=bank: e.transpose(
                            p_m[b][:, 128 * q:128 * q + 128],
                            s[:, 128 * pp_:128 * pp_ + 128], ident[:, :]), inc="mm")
                    t_tp = S.emitted["mm"]
                    if src_i == 0:
                        S.wait(A_, "mm", t_tp)
                        S.wait(A_, "Gstg", t_stg_free)
                        for q in range(4):
                            pp_ = 4 * half + q
                            S.op(A_, lambda e, q=q, pp_=pp_: e.copy(
                                stg[:, 128 * pp_:128 * pp_ + 128],
                                p_m[0][:, 128 * q:128 * q + 128]), inc="ss")
                        pm_ev[0] = ("ss", S.emitted["ss"])
                    else:
                        S.wait(V, "mm", t_tp)
                        for q in range(4):
                            pp_ = 4 * half + q
                            S.op(V, lambda e, q=q, pp_=pp_, j=j: e.tensor_scalar(
                                xn2v(pp_)[:, 128 * j:128 * j + 128],
                                p_m[1][:, 128 * q:128 * q + 128],
                                cols[:, 80 + pp_:81 + pp_], cols[:, 88 + pp_:89 + pp_],
                                ALU.mult, ALU.add), inc="vv")
                        pm_ev[1] = ("vv", S.emitted["vv"])
            t_tile_trans = S.emitted["mm"]
            S.wait(G, "ss", pm_ev[0][1])
            t_stg_free = S.dma(G, s_xnT.ap()[j], stg, stream="Gstg")
        else:
            S.wait(G, "vv", t_apply)
            S.dma(G, s_row.ap()[0:1, :], xn_tok[0:1, :], stream="Gmisc")
            t_rw = S.dma(G, s_row.ap()[1:2, :], xn2_tok[0:1, :], stream="Gmisc")
            S.wait(G, "Gmisc", t_rw)
            for pp_ in range(NP):
                S.dma(G, xn2v(pp_)[:, Tc:Tc + 1],
                      s_row.ap()[1].rearrange("(p j) -> j p", j=P)[:, pp_:pp_ + 1],
                      stream="Gmisc", slow=True)
            t_sm = S.emitted["Gmisc"]
            S.wait(V, "Gmisc", t_sm)
            for pp_ in range(NP):
                S.op(V, lambda e, pp_=pp_: e.tensor_scalar(
                    xn2v(pp_)[:, Tc:Tc + 1], xn2v(pp_)[:, Tc:Tc + 1],
                    cols[:, 80 + pp_:81 + pp_], cols[:, 88 + pp_:89 + pp_],
                    ALU.mult, ALU.add), inc="vv")
    S.drain(V)
    for pp_ in range(NP):
        S.op(V, lambda e, pp_=pp_: e.tensor_scalar(
            xn2v(pp_)[:, 0:1], xn2v(pp_)[:, 0:1], pmask[:, 0:1], None,
            ALU.mult, ALU.bypass), inc="vv")
    S.barrier()

    if _stopped("S1"):
        return _finish(nc, S, ctx)
    def mk_evac_w(st):
        def evac_w(dd, bank, t_mm):
            S.wait(V, "mm", t_mm)
            if dd % 4 == 0:
                S.wait(V, "Gsp", scrE_t[0])
            sl = scrE[:, 512 * (dd % 4):512 * (dd % 4) + 512]
            S.op(V, lambda e, b=bank, sl=sl, dd=dd: e.tensor_scalar(
                sl, p_proj[b][:, :], cols[:, 64 + dd:65 + dd], 10.0,
                ALU.add, ALU.min), inc="vv")
            S.drain(V)
            S.op(V, lambda e, sl=sl: e.tensor_scalar(
                sl, sl, -10.0, None, ALU.max, ALU.bypass), inc="vv")
            S.drain(V)
            set_bank_free(bank, "vv", S.emitted["vv"])
            if dd % 4 == 3:
                t_v = S.emitted["vv"]
                S.wait(A_, "vv", t_v)
                S.op(A_, lambda e: e.activation(scrE[:, :], scrE[:, :], AF.Exp), inc="ss")
                t_e = S.emitted["ss"]
                S.wait(G, "ss", t_e)
                _set_scrE(spill_batch(spill["pexp"], st, dd))
        return evac_w

    def mk_evac_k(st):
        def evac_k(dd, bank, t_mm):
            S.wait(A_, "mm", t_mm)
            if dd % 4 == 0:
                S.wait(A_, "Gsp", scrE_t[0])
            S.op(A_, lambda e, b=bank, dd=dd: e.copy(
                scrE[:, 512 * (dd % 4):512 * (dd % 4) + 512], p_proj[b][:, :]), inc="ss")
            set_bank_free(bank, "ss", S.emitted["ss"])
            if dd % 4 == 3:
                S.wait(G, "ss", S.emitted["ss"])
                _set_scrE(spill_batch(spill["k"], st, dd))
        return evac_k

    t_vtb = [0, 0]
    S.sem("Gvt0")
    S.sem("Gvt1")

    def mk_evac_v(st):
        def evac_v(dd, bank, t_mm):
            S.wait(A_, "mm", t_mm)
            S.op(A_, lambda e, b=bank: e.copy(vscr, p_proj[b][:, :]), inc="ss")
            t_vs = S.emitted["ss"]
            set_bank_free(bank, "ss", t_vs)
            S.wait(TE, "ss", t_vs)
            sem, val = pm_ev[0]
            S.wait(TE, sem, val)
            for tt in range(4):
                S.op(TE, lambda e, tt=tt: e.transpose(
                    p_m[0][:, 128 * tt:128 * tt + 128],
                    vscr[:, 128 * tt:128 * tt + 128], ident[:, :]), inc="mm")
            t_tp = S.emitted["mm"]
            S.wait(A_, "mm", t_tp)
            S.wait(A_, "Gvt%d" % (dd % 2), t_vtb[dd % 2])
            for tt in range(4):
                S.op(A_, lambda e, tt=tt, dd=dd: e.copy(
                    vtb[dd % 2][:, 128 * tt:128 * tt + 128],
                    p_m[0][:, 128 * tt:128 * tt + 128]), inc="ss")
            pm_ev[0] = ("ss", S.emitted["ss"])
            S.wait(G, "ss", S.emitted["ss"])
            t_vtb[dd % 2] = S.dma(
                G,
                s_vt.ap()[4 * st:4 * st + 4][:, :, 128 * dd:128 * dd + 128].rearrange("c i q -> i c q"),
                vtb[dd % 2].rearrange("i (c q) -> i c q", q=128),
                stream="Gvt%d" % (dd % 2))
        return evac_v

    def mk_evac_sig(st, dest):
        def evac_sig(dd, bank, t_mm):
            S.wait(A_, "mm", t_mm)
            if dd % 4 == 0:
                S.wait(A_, "Gsp", scrE_t[0])
            S.op(A_, lambda e, b=bank, dd=dd: e.activation(
                scrE[:, 512 * (dd % 4):512 * (dd % 4) + 512],
                p_proj[b][:, :], AF.Sigmoid), inc="ss")
            set_bank_free(bank, "ss", S.emitted["ss"])
            if dd % 4 == 3:
                S.wait(G, "ss", S.emitted["ss"])
                _set_scrE(spill_batch(dest, st, dd))
        return evac_sig

    units = []
    for st in range(NST):
        units.append(((24, 56, xn2v, st), [("ww", mk_evac_w(st))]))
    for st in range(NST):
        units.append(((0, 32, xn2v, st), [("wk", mk_evac_k(st))]))
    for st in range(NST):
        units.append(((8, 40, xn2v, st), [("wv", mk_evac_v(st))]))
    for st in range(NST):
        units.append(((16, 48, xn2v, st),
                      [("wr", mk_evac_sig(st, spill["r"])),
                       ("wg", mk_evac_sig(st, spill["g"]))]))
    fin_after = {3: ["ww"], 7: ["wk"], 11: ["wv"], 15: ["wr", "wg"]}

    mixh = {0: emit_mix_bg(units[0][0][3], units[0][0][0], units[0][0][1],
                           units[0][0][2], 0, [0])}
    for k, (mspec, mmsets) in enumerate(units):
        if k + 1 < len(units):
            sig2, oms2, src2, st2 = units[k + 1][0]
            mixh[k + 1] = emit_mix_bg(st2, sig2, oms2, src2, (k + 1) % 2,
                                      [mx_used[(k + 1) % 2]])
        for wn, ecb in mmsets:
            emit_matmuls(wn, mxh[k % 2], mxl[k % 2], mixh[k], ecb)
        mx_used[k % 2] = S.emitted["mm"]
        for wn in fin_after.get(k, []):
            finish_weight(wn)
        pump_weights()
    S.barrier()

    if _stopped("S7a"):
        return _finish(nc, S, ctx)
    pexp_p = bigA[:, 0:Tc]
    k_p = bigA[:, Tc:2 * Tc]
    r_p = bigA[:, 2 * Tc:3 * Tc]
    C_p = bigA[:, 3 * Tc:4 * Tc]
    sA = bigA[:, 4 * Tc:5 * Tc]
    sB = bigA[:, 5 * Tc:6 * Tc]
    sU = bigA[:, 6 * Tc:7 * Tc]
    vt_all = bigA[:, 7 * Tc:8 * Tc]
    ktilT_all = sA
    shist = sB
    S.op(V, lambda e: e.memset(s_pp[0][:, :], 0.0), inc="vv")
    S.drain(V)
    t_spp0_ms = S.emitted["vv"]
    t_df_used = 0
    t_sl_dma = 0
    t_u2_done = 0
    pa_free = [("vv", 0), ("vv", 0)]
    t_pexp_free = 0
    t_k_free = 0
    t_r_free = 0
    t_cp_free = 0
    for p in range(NP):
        S.wait(SY, "vv", t_pexp_free)
        t_lp = S.dma(SY, pexp_p, spill["pexp"].ap()[p], stream="Lp")
        S.wait(SY, "vv", t_k_free)
        t_lk = S.dma(SY, k_p, spill["k"].ap()[p], stream="Lk2")
        S.wait(SY, "vv", t_r_free)
        t_lr = S.dma(SY, r_p, spill["r"].ap()[p], stream="Lr2")
        S.wait(SY, "mm", t_u2_done)
        t_vta = S.dma(SY, vt_all.rearrange("i (c q) -> i c q", q=128),
                      s_vt.ap()[:, :, 128 * p:128 * p + 128].rearrange("c i q -> i c q"),
                      stream="Lvt0")
        S.wait(V, "Lp", t_lp)
        S.wait(V, "vv", t_cp_free)
        for c in range(NCH):
            S.op(V, lambda e, c=c: e.tensor_tensor_scan(
                C_p[:, L * c:L * c + L], pexp_p[:, L * c:L * c + L],
                pexp_p[:, L * c:L * c + L], 0.0, ALU.add, ALU.bypass), inc="vv")
        S.drain(V)
        S.op(V, lambda e, p=p: e.tensor_copy(
            cend_sb[:, NCH * p:NCH * p + NCH], C_p[:, L - 1::L]), inc="vv")
        S.drain(V)
        S.op(V, lambda e, p=p: e.tensor_reduce(
            cols[:, 144 + p:145 + p], cend_sb[:, NCH * p:NCH * p + NCH],
            AX.X, ALU.add), inc="vv")
        S.op(V, lambda e: e.tensor_tensor(sA, pexp_p, C_p, ALU.subtract), inc="vv")
        t_v1 = S.emitted["vv"]
        t_pexp_free = t_v1
        S.wait(A_, "vv", t_v1)
        S.wait(A_, "Gsl", t_sl_dma)
        S.op(A_, lambda e: e.activation(sB, sA, AF.Exp), inc="ss")
        t_s1 = S.emitted["ss"]
        S.wait(V, "ss", t_s1)
        S.wait(V, "Lr2", t_lr)
        S.op(V, lambda e: e.tensor_tensor(sA, r_p, sB, ALU.mult), inc="vv")
        t_rt = S.emitted["vv"]
        S.wait(G, "vv", t_rt)
        t_rt_dma = S.dma(G, spill["rt"].ap()[p], sA, stream="Grt")
        S.drain(V)
        S.op(V, lambda e: e.tensor_scalar(sB, C_p, CLAMP, None, ALU.min, ALU.bypass), inc="vv")
        t_v2 = S.emitted["vv"]
        S.wait(A_, "vv", t_v2)
        S.op(A_, lambda e: e.activation(sU, sB, AF.Exp), inc="ss")
        t_s2 = S.emitted["ss"]
        S.wait(V, "ss", t_s2)
        S.wait(V, "Lk2", t_lk)
        S.op(V, lambda e: e.tensor_tensor(sB, k_p, sU, ALU.mult), inc="vv")
        t_kh = S.emitted["vv"]
        S.wait(G, "vv", t_kh)
        t_kh_dma = S.dma(G, spill["khat"].ap()[p], sB, stream="Gkh")
        S.drain(V)
        S.op(V, lambda e, p=p: e.scalar_tensor_tensor(
            sU, k_p, cols[:, 72 + p:73 + p], r_p, ALU.mult, ALU.mult), inc="vv")
        t_tmp = S.emitted["vv"]
        t_r_free = t_tmp
        S.wait(TE, "vv", t_tmp)
        sem, val = pm_ev[1]
        S.wait(TE, sem, val)
        for c in range(NCH):
            S.op(TE, lambda e, c=c: e.matmul(
                p_m[1][:, 2 * c:2 * c + 2], sU[:, L * c:L * c + L], onesb[:, :],
                start=True, stop=True), inc="mm")
        t_dv = S.emitted["mm"]
        S.wait(V, "mm", t_dv)
        S.op(V, lambda e, p=p: e.tensor_copy(
            dvals_sb[:, :].rearrange("i (c h) -> i c h", h=H)[:, :, 2 * p:2 * p + 2],
            p_m[1][:, 0:32].rearrange("i (c h) -> i c h", h=2)), inc="vv")
        S.drain(V)
        t_dvev = S.emitted["vv"]
        pm_ev[1] = ("vv", t_dvev)
        S.wait(V, "Grt", t_rt_dma)
        for c in range(NCH):
            S.op(V, lambda e, c=c, p=p: e.tensor_scalar(
                sA[:, L * c:L * c + L], C_p[:, L * c:L * c + L],
                cend_sb[:, NCH * p + c:NCH * p + c + 1], None,
                ALU.subtract, ALU.bypass), inc="vv")
        t_v3 = S.emitted["vv"]
        t_cp_free = t_v3
        S.wait(A_, "vv", t_v3)
        S.wait(A_, "Gkh", t_kh_dma)
        S.op(A_, lambda e: e.activation(sB, sA, AF.Exp), inc="ss")
        t_s3 = S.emitted["ss"]
        S.wait(V, "ss", t_s3)
        S.wait(V, "mm", t_dv)
        S.drain(V)
        S.op(V, lambda e: e.tensor_tensor(sU, k_p, sB, ALU.mult), inc="vv")
        t_ktil = S.emitted["vv"]
        t_k_free = t_ktil
        S.op(A_, lambda e, p=p: e.activation(
            dfc[:, :], cend_sb[:, NCH * p:NCH * p + NCH], AF.Exp, scale=-1.0), inc="ss")
        t_dfc = S.emitted["ss"]

        S.wait(V, "mm", S.emitted["mm"])
        S.drain(V)
        S.op(V, lambda e: e.memset(shist[:, 0:128], 0.0), inc="vv")
        S.drain(V)
        ktt_ss = {}
        for c in range(NCH):
            bank01 = (c // 4) % 2
            if c % 4 == 0:
                sem, val = pm_ev[bank01]
                S.wait(TE, sem, val)
                if c == 0:
                    S.wait(TE, "vv", t_ktil)
            S.op(TE, lambda e, c=c, b=bank01: e.transpose(
                p_m[b][:, 128 * (c % 4):128 * (c % 4) + 128],
                sU[:, L * c:L * c + L], ident[:, :]), inc="mm")
            if c % 4 == 3:
                t_tp = S.emitted["mm"]
                S.wait(A_, "mm", t_tp)
                for c2 in range(c - 3, c + 1):
                    S.op(A_, lambda e, c2=c2, b=bank01: e.copy(
                        ktilT_all[:, 128 * c2:128 * c2 + 128],
                        p_m[b][:, 128 * (c2 % 4):128 * (c2 % 4) + 128]), inc="ss")
                pm_ev[bank01] = ("ss", S.emitted["ss"])
                ktt_ss[c // 4] = S.emitted["ss"]
        S.wait(TE, "Lvt0", t_vta)
        for c in range(NCH):
            bank01 = (c // 4) % 2
            if c % 4 == 0:
                S.wait(TE, "ss", ktt_ss[c // 4])
                sem, val = pa_free[bank01]
                S.wait(TE, sem, val)
            S.op(TE, lambda e, c=c, b=bank01: e.matmul(
                p_a[b][:, 128 * (c % 4):128 * (c % 4) + 128],
                ktilT_all[:, 128 * c:128 * c + 128],
                vt_all[:, 128 * c:128 * c + 128],
                start=(c % 4 == 0), stop=(c % 4 == 3)), inc="mm")
            if c % 4 == 3:
                u2_mm = S.emitted["mm"]
                t_u2_done = u2_mm
                S.wait(V, "mm", u2_mm)
                if c == 3:
                    S.wait(V, "ss", t_dfc)
                for c2 in range(c - 3, c + 1):
                    dst = shist[:, 128 * (c2 + 1):128 * (c2 + 1) + 128] \
                        if c2 < NCH - 1 else s_pp[0][:, 128 * p:128 * p + 128]
                    srcb = shist[:, 128 * c2:128 * c2 + 128]
                    for hh in range(2):
                        b0 = 64 * hh
                        S.op(V, lambda e, c2=c2, b0=b0, dst=dst, srcb=srcb, b=bank01: e.scalar_tensor_tensor(
                            dst[b0:b0 + 64, b0:b0 + 64],
                            srcb[b0:b0 + 64, b0:b0 + 64],
                            dfc[b0:b0 + 64, c2:c2 + 1],
                            p_a[b][b0:b0 + 64, 128 * (c2 % 4) + b0:128 * (c2 % 4) + b0 + 64],
                            ALU.mult, ALU.add), inc="vv")
                    S.drain(V)
                pa_free[bank01] = ("vv", S.emitted["vv"])
        t_df_used = S.emitted["vv"]
        S.wait(G, "vv", t_df_used)
        if p == 0:
            S.wait(G, "vv", t_spp0_ms)
        t_sl_dma = S.dma(G, s_sloc.ap()[:, :, 128 * p:128 * p + 128].rearrange("c i q -> i c q"),
                         shist.rearrange("i (c q) -> i c q", q=128), stream="Gsl")
    S.wait(G, "vv", S.emitted["vv"])
    t_ci1 = S.dma(G, cc_in.ap()[:, 0:D], s_pp[0][:, :], stream="Gcc")
    t_ci2 = S.dma(G, cc_in.ap()[:, D:D + NP], cols[:, 144:152], stream="Gcc")
    S.wait(G, "Gcc", t_ci2)
    S.op(G, lambda e: e.collective_compute(
        "AllGather", ALU.bypass, replica_groups=[list(range(NCORES))],
        ins=[cc_in.ap().opt()], outs=[cc_out.ap().opt()]), inc="cc")
    S.barrier()

    if _stopped("S5"):
        return _finish(nc, S, ctx)
    xnv = lambda p: bigA[:, p * (Tc + 1):(p + 1) * (Tc + 1)]
    for p in range(NP):
        S.dma(SY, xnv(p)[:, 0:Tc].rearrange("i (j q) -> i j q", q=P),
              s_xnT.ap()[:, :, 128 * p:128 * p + 128].rearrange("j i q -> i j q"),
              stream="Lc2")
        S.dma(SY, xnv(p)[:, Tc:Tc + 1],
              s_row.ap()[0].rearrange("(pp j) -> j pp", j=P)[:, p:p + 1],
              stream="Lc2", slow=True)
    S.barrier()
    mix_rf = {0: emit_mix(0, 96, 128, xnv)}
    mix_kf = {}
    for st in range(NST):
        if st + 1 < NST:
            mix_rf[st + 1] = emit_mix_bg(st + 1, 96, 128, xnv, (st + 1) % 2,
                                         [mx_used[(st + 1) % 2]])
        else:
            mix_kf[0] = emit_mix_bg(0, 104, 136, xnv, 0, [mx_used[0]])
        def evac_rf(dd, bank, t_mm):
            S.wait(A_, "mm", t_mm)
            if dd % 4 == 0:
                S.wait(A_, "Gsp", scrE_t[0])
            S.op(A_, lambda e, b=bank, dd=dd: e.activation(
                scrE[:, 512 * (dd % 4):512 * (dd % 4) + 512],
                p_proj[b][:, :], AF.Sigmoid), inc="ss")
            set_bank_free(bank, "ss", S.emitted["ss"])
            if dd % 4 == 3:
                S.wait(G, "ss", S.emitted["ss"])
                _set_scrE(spill_batch(spill["rf"], st, dd))
        emit_matmuls("fr", mxh[st % 2], mxl[st % 2], mix_rf[st], evac_rf)
        mx_used[st % 2] = S.emitted["mm"]
        pump_weights()
    finish_weight("fr")
    S.wait(SY, "Gsp", S.emitted["Gsp"])

    xfst = scrE[:, 0:1024]
    xf_state = [xfst[:, 0:512], xfst[:, 512:1024]]
    xf_llc = [cols[:, 112:120], cols[:, 120:128]]
    smask_ps = p_yt
    S.op(V, lambda e: e.memset(s_pp[0][:, :], 0.0), inc="vv")
    S.op(V, lambda e: e.memset(s_pp[1][:, :], 0.0), inc="vv")
    S.drain(V)
    xf_lg = {}
    xf_free = [("vv", 0), ("vv", 0)]
    xf_done = [0]

    def xf_load(m):
        sem, val = xf_free[0]
        S.wait(SY, sem, val)
        if m == 0:
            S.wait(SY, "cc", 1)
        S.dma(SY, xfst[:, 0:1024], cc_out.ap()[m][:, 0:1024], stream="Lgb0")
        xf_lg[m] = S.dma(SY, xf_llc[m % 2], cc_out.ap()[m][:, 1024:1032],
                         stream="Lgb0")

    def xf_fold(m):
        buf = m % 2
        S.wait(V, "Lgb0", xf_lg[m])
        S.op(V, lambda e, b=buf, m=m: e.tensor_tensor(
            lmaskc[:, :], xf_llc[b],
            cmask[:, m:m + 1].to_broadcast([P, NP]), ALU.mult), inc="vv")
        t_lm = S.emitted["vv"]
        S.wait(A_, "vv", t_lm)
        S.op(A_, lambda e: e.activation(dtilc[:, :], lmaskc[:, :], AF.Exp, scale=-1.0), inc="ss")
        t_dt = S.emitted["ss"]
        for hf in range(2):
            S.op(V, lambda e, m=m, hf=hf: e.tensor_tensor(
                smask_ps[:, 512 * hf:512 * hf + 512], xf_state[hf][:, :],
                cmask[:, m:m + 1].to_broadcast([P, 512]), ALU.mult), inc="vv")
        S.drain(V)
        xf_free[0] = ("vv", S.emitted["vv"])
        if m + 1 < NCORES:
            xf_load(m + 1)
        S.wait(V, "ss", t_dt)
        for p in range(NP):
            for hh in range(2):
                b0 = 64 * hh
                S.op(V, lambda e, m=m, p=p, b0=b0: e.scalar_tensor_tensor(
                    s_pp[(m + 1) % 2][b0:b0 + 64, 128 * p + b0:128 * p + b0 + 64],
                    s_pp[m % 2][b0:b0 + 64, 128 * p + b0:128 * p + b0 + 64],
                    dtilc[b0:b0 + 64, p:p + 1],
                    smask_ps[b0:b0 + 64, 128 * p + b0:128 * p + b0 + 64],
                    ALU.mult, ALU.add), inc="vv")
        S.drain(V)
        if m == NCORES - 1:
            xf_done[0] = S.emitted["vv"]

    khat_bd = [bigA[:, 0:2048], bigA[:, 2048:4096]]
    rt_b = [bigA[:, 4096:5120], bigA[:, 5120:6144]]
    vt_b = [bigA[:, 6144:7168], bigA[:, 7168:8192]]
    sloc_b = [bigA[:, 8192:9216], bigA[:, 9216:10240]]
    yt_b = bigA[:, 10240:11264]
    am_b = bigA[:, 11264:13312]
    g2 = [bigA[:, 13312:14336], bigA[:, 14336:15360]]
    ygh = mxh
    ygl = mxl
    out_tok = [bigC[:, 1024 * i:1024 * i + 1024] for i in range(4)]
    sum_scr = bigC[:, 4096:4608]
    ffn_str = [bigC[:, 4608:5120], bigC[:, 5120:5632]]
    gf4 = bigC[:, 5632:6144]
    S.sem("Gout")
    t_par_mm = [0, 0]
    t_par_vv = [0, 0]
    t_pa_ev = [("vv", 0), ("vv", 0)]
    t_ffn_free = [("vv", 0), ("vv", 0)]
    tk = {}
    b_t = {"yg_used": [0, 0], "out_free": 0, "am_used": 0, "yt_ev": 0,
           "lloc": 0, "xpre": 0, "yg0": None}

    def load_chunk(cn, eng=SY):
        q = cn % 2
        pfx = "L" if eng is SY else "P"
        S.wait(eng, "mm", t_par_mm[q])
        S.wait(eng, "vv", t_par_vv[q])
        S.dma(eng,
              khat_bd[q][0:64, :].rearrange("i (p tt) -> i p tt", tt=256)[:, :, 0:128],
              spill["khat"].ap()[:, 0:64, L * cn:L * cn + L].rearrange("p i t -> i p t"),
              stream=pfx + "kh%d" % q)
        t_kh = S.dma(eng,
                     khat_bd[q][64:128, :].rearrange("i (p tt) -> i p tt", tt=256)[:, :, 128:256],
                     spill["khat"].ap()[:, 64:128, L * cn:L * cn + L].rearrange("p i t -> i p t"),
                     stream=pfx + "kh%d" % q)
        t_rt = S.dma(eng, rt_b[q].rearrange("i (p t) -> i p t", t=L),
                     spill["rt"].ap().rearrange("p i t -> i p t")[:, :, L * cn:L * cn + L],
                     stream=pfx + "rt%d" % q)
        t_vt = S.dma(eng, vt_b[q], s_vt.ap()[cn], stream=pfx + "vtb%d" % q)
        t_sl = S.dma(eng, sloc_b[q], s_sloc.ap()[cn], stream=pfx + "slb%d" % q)
        t_g = S.dma(eng, g2[q].rearrange("i (p t) -> i p t", t=L),
                    spill["g"].ap().rearrange("p i t -> i p t")[:, :, L * cn:L * cn + L],
                    stream=pfx + "g%d" % q)
        return {"kh": (pfx + "kh%d" % q, t_kh), "rt": (pfx + "rt%d" % q, t_rt),
                "vt": (pfx + "vtb%d" % q, t_vt), "sl": (pfx + "slb%d" % q, t_sl),
                "g": (pfx + "g%d" % q, t_g)}

    def b_prologue():
        S.wait(V, "ss", S.emitted["ss"])
        S.op(V, lambda e: e.memset(bigA[:, 0:4096], 0.0), inc="vv")
        S.op(V, lambda e: e.memset(s_inb[:, :], 0.0), inc="vv")
        S.drain(V)
        t_bms = S.emitted["vv"]
        t_par_vv[0] = t_bms
        t_par_vv[1] = t_bms
        tk[0] = load_chunk(0)
        tk[1] = load_chunk(1)
        b_t["xpre"] = S.dma(G, out_hbm[:, :], x_in[1:Tc + 1, :], stream="Gout")
        b_t["lloc"] = xf_done[0]
        b_t["yt_ev"] = xf_done[0]

    def b_chunk(c):
        st_, cc_ = c // 4, c % 4
        q = c % 2
        if c + 1 < NCH and c + 1 not in tk:
            tk[c + 1] = load_chunk(c + 1)
        t_kh, t_rt, t_vt, t_sl, t_g = (
            tk[c]["kh"], tk[c]["rt"], tk[c]["vt"], tk[c]["sl"], tk[c]["g"])
        S.wait(A_, "vv", b_t["lloc"])
        S.op(A_, lambda e: e.activation(plocc[:, :], cols[:, 152:160], AF.Exp, scale=-1.0), inc="ss")
        t_pl = S.emitted["ss"]
        S.wait(V, "ss", t_pl)
        S.wait(V, t_sl[0], t_sl[1])
        for p in range(NP):
            for hh in range(2):
                b0 = 64 * hh
                S.op(V, lambda e, p=p, b0=b0, q=q: e.scalar_tensor_tensor(
                    s_inb[b0:b0 + 64, 128 * p + b0:128 * p + b0 + 64],
                    s_pp[0][b0:b0 + 64, 128 * p + b0:128 * p + b0 + 64],
                    plocc[b0:b0 + 64, p:p + 1],
                    sloc_b[q][b0:b0 + 64, 128 * p + b0:128 * p + b0 + 64],
                    ALU.mult, ALU.add), inc="vv")
        t_sin = S.emitted["vv"]
        S.op(V, lambda e, c=c: e.tensor_tensor(
            cols[:, 152:160], cols[:, 152:160], cend_sb[:, c::NCH], ALU.add), inc="vv")
        S.drain(V)
        b_t["lloc"] = S.emitted["vv"]
        S.wait(TE, t_kh[0], t_kh[1])
        S.wait(TE, t_rt[0], t_rt[1])
        for grp in range(4):
            bank = grp % 2
            sem, val = t_pa_ev[bank]
            S.wait(TE, sem, val)
            for hh in range(4):
                h = 4 * grp + hh
                p = h // 2
                S.op(TE, lambda e, h=h, p=p, bank=bank, hh=hh, q=q: e.matmul(
                    p_a[bank][:, 128 * hh:128 * hh + 128],
                    khat_bd[q][:, 128 * h:128 * h + 128],
                    rt_b[q][:, 128 * p:128 * p + 128],
                    start=(hh == 0), stop=(hh == 3)), inc="mm")
            t_amm = S.emitted["mm"]
            S.wait(V, "mm", t_amm)
            S.wait(V, "mm", b_t["am_used"])
            S.op(V, lambda e, grp=grp, bank=bank: e.tensor_tensor(
                am_b[:, 512 * grp:512 * grp + 512], p_a[bank][:, :],
                maska[:, :], ALU.mult), inc="vv")
            t_pa_ev[bank] = ("vv", S.emitted["vv"])
        t_amask = S.emitted["vv"]
        S.wait(TE, "vv", t_amask)
        S.wait(TE, "vv", t_sin)
        S.wait(TE, t_vt[0], t_vt[1])
        S.wait(TE, "vv", b_t["yt_ev"])
        for b in range(2):
            for hh in range(8):
                h = 8 * b + hh
                S.op(TE, lambda e, h=h, b=b, hh=hh, q=q: e.matmul(
                    p_yt[:, 512 * b + 64 * hh: 512 * b + 64 * hh + 64],
                    am_b[:, 128 * h:128 * h + 128],
                    vt_b[q][:, 64 * h:64 * h + 64],
                    start=(hh == 0), stop=False), inc="mm")
            for pp_ in range(4):
                p = 4 * b + pp_
                S.op(TE, lambda e, p=p, b=b, pp_=pp_, q=q: e.matmul(
                    p_yt[:, 512 * b + 128 * pp_: 512 * b + 128 * pp_ + 128],
                    rt_b[q][:, 128 * p:128 * p + 128],
                    s_inb[:, 128 * p:128 * p + 128],
                    start=False, stop=(pp_ == 3)), inc="mm")
        t_ymm = S.emitted["mm"]
        b_t["am_used"] = t_ymm
        t_par_mm[q] = t_ymm
        S.wait(V, "mm", t_ymm)
        for h in range(H):
            S.op(V, lambda e, h=h, c=c, q=q: e.scalar_tensor_tensor(
                yt_b[:, 64 * h:64 * h + 64], vt_b[q][:, 64 * h:64 * h + 64],
                dvals_sb[:, c * H + h:c * H + h + 1],
                p_yt[:, 64 * h:64 * h + 64], ALU.mult, ALU.add), inc="vv")
        t_yev = S.emitted["vv"]
        b_t["yt_ev"] = t_yev
        S.wait(TE, "vv", t_yev)
        S.wait(V, t_g[0], t_g[1])
        if c == 0:
            S.wait(V, "Gff0", S.emitted["Gff0"])
            S.wait(V, "Gff1", S.emitted["Gff1"])
        ybuf = st_ % 2
        for grp in range(2):
            sem, val = pm_ev[grp]
            S.wait(TE, sem, val)
            for i4 in range(4):
                p = 4 * grp + i4
                S.op(TE, lambda e, p=p, g2_=grp, i4=i4: e.transpose(
                    p_m[g2_][:, 128 * i4:128 * i4 + 128],
                    yt_b[:, 128 * p:128 * p + 128], ident[:, :]), inc="mm")
            t_tp = S.emitted["mm"]
            S.wait(V, "mm", t_tp)
            gsl = g2[q][:, 512 * grp:512 * grp + 512]
            ysl = ygh[ybuf][:, :].rearrange("i (p t) -> i p t", t=TCH)[
                :, 4 * grp:4 * grp + 4, 128 * cc_:128 * cc_ + 128]
            ysll = ygl[ybuf][:, :].rearrange("i (p t) -> i p t", t=TCH)[
                :, 4 * grp:4 * grp + 4, 128 * cc_:128 * cc_ + 128]
            S.op(V, lambda e, g2_=grp, gsl=gsl: e.tensor_tensor(
                gf4, p_m[g2_][:, :], gsl, ALU.mult), inc="vv")
            S.drain(V)
            t_gf = S.emitted["vv"]
            pm_ev[grp] = ("vv", t_gf)
            S.wait(A_, "vv", t_gf)
            S.op(A_, lambda e, ysl=ysl: e.copy(
                ysl, gf4.rearrange("i (p t) -> i p t", t=128)), inc="ss")
            t_h = S.emitted["ss"]
            S.wait(V, "ss", t_h)
            S.op(V, lambda e, ysl=ysl, ysll=ysll: e.tensor_tensor(
                ysll, gf4.rearrange("i (p t) -> i p t", t=128), ysl,
                ALU.subtract), inc="vv")
            S.drain(V)
        t_par_vv[q] = S.emitted["vv"]


    t_rfs_used = [("vv", 0), ("vv", 0)]
    t_ffnw = [0, 0]
    S.sem("Gff0")
    S.sem("Gff1")
    xf_m = [0]

    def xf_next():
        if xf_m[0] >= NCORES:
            return
        if xf_m[0] == 0:
            xf_load(0)
        xf_fold(xf_m[0])
        xf_m[0] += 1

    for st in range(NST):
        t_mix = mix_kf[st] if st in mix_kf else emit_mix(st, 104, 136, xnv)
        if st >= 1:
            xf_next()
        kbuf = 1 - st % 2

        def evac_kf(dd, bank, t_mm, kbuf=kbuf):
            S.wait(V, "mm", t_mm)
            if dd == 0:
                S.wait(V, "mm", mx_used[kbuf])
                S.wait(A_, "mm", mx_used[kbuf])
            S.op(V, lambda e, b=bank: e.tensor_scalar(
                relscr, p_proj[b][:, :], 0.0, None, ALU.max, ALU.bypass), inc="vv")
            S.drain(V)
            S.op(V, lambda e: e.tensor_tensor(sqf, relscr, relscr, ALU.mult), inc="vv")
            S.drain(V)
            t_q = S.emitted["vv"]
            set_bank_free(bank, "vv", t_q)
            S.wait(A_, "vv", t_q)
            S.op(A_, lambda e, dd=dd, kbuf=kbuf: e.copy(
                mxh[kbuf][:, TCH * dd:TCH * dd + TCH], sqf), inc="ss")
            t_h = S.emitted["ss"]
            S.wait(V, "ss", t_h)
            S.op(V, lambda e, dd=dd, kbuf=kbuf: e.tensor_tensor(
                mxl[kbuf][:, TCH * dd:TCH * dd + TCH], sqf,
                mxh[kbuf][:, TCH * dd:TCH * dd + TCH], ALU.subtract), inc="vv")
            S.drain(V)
        emit_matmuls("fk", mxh[st % 2], mxl[st % 2], t_mix, evac_kf)
        mx_used[st % 2] = S.emitted["mm"]
        pump_weights()
        t_ksq = (S.emitted["ss"], S.emitted["vv"])
        if st >= 1:
            xf_next()
        if st == NST - 1:
            b_prologue()

        def evac_fv(dd, bank, t_mm):
            sem, val = t_rfs_used[dd % 2]
            S.wait(SY, sem, val)
            t_rfs = S.dma(SY, rfstr[dd % 2],
                          spill["rf"].ap()[dd][:, TCH * st:TCH * st + TCH],
                          stream="Lrf%d" % (dd % 2))
            S.wait(V, "mm", t_mm)
            S.wait(V, "Lrf%d" % (dd % 2), t_rfs)
            S.wait(V, "Gff%d" % (dd % 2), t_ffnw[dd % 2])
            S.op(V, lambda e, b=bank, dd=dd: e.tensor_tensor(
                fb[dd % 2], p_proj[b][:, :], rfstr[dd % 2], ALU.mult), inc="vv")
            S.drain(V)
            t_fb = S.emitted["vv"]
            t_rfs_used[dd % 2] = ("vv", t_fb)
            set_bank_free(bank, "vv", t_fb)
            S.wait(G, "vv", t_fb)
            t_ffnw[dd % 2] = S.dma(
                G, spill["ffn"].ap()[dd][:, TCH * st:TCH * st + TCH],
                fb[dd % 2], stream="Gff%d" % (dd % 2))
        emit_matmuls("fv", mxh[kbuf], mxl[kbuf], t_ksq, evac_fv)
        mx_used[kbuf] = S.emitted["mm"]
        pump_weights()
        if st >= 1:
            xf_next()
        if st == NST - 1:
            b_t["yg_used"][0] = S.emitted["mm"]
            S.wait(V, "mm", b_t["yg_used"][0])
            S.wait(A_, "mm", b_t["yg_used"][0])
            for _c in range(4):
                S.pump_bg(4)
                b_chunk(_c)
            b_t["yg0"] = (S.emitted["ss"], S.emitted["vv"])
    while xf_m[0] < NCORES:
        xf_next()
    finish_weight("fk")
    finish_weight("fv")
    S.barrier()

    if _stopped("S6"):
        return _finish(nc, S, ctx)

    if _stopped("XF"):
        return _finish(nc, S, ctx)
    for st in range(NST):
        if st == 0 and b_t["yg0"] is not None:
            t_yg = b_t["yg0"]
        else:
            S.wait(V, "mm", b_t["yg_used"][st % 2])
            S.wait(A_, "mm", b_t["yg_used"][st % 2])
            for cc_ in range(4):
                S.pump_bg(4)
                b_chunk(4 * st + cc_)
            t_yg = (S.emitted["ss"], S.emitted["vv"])
        if _stopped("B_tr"):
            return _finish(nc, S, ctx)

        def evac_wo(dd, bank, t_mm):
            sem, val = t_ffn_free[dd % 2]
            S.wait(SY, sem, val)
            t_ff = S.dma(SY, ffn_str[dd % 2],
                         spill["ffn"].ap()[dd][:, TCH * st:TCH * st + TCH],
                         stream="Lfn%d" % (dd % 2))
            S.wait(V, "mm", t_mm)
            S.wait(V, "Lfn%d" % (dd % 2), t_ff)
            S.op(V, lambda e, b=bank, dd=dd: e.scalar_tensor_tensor(
                sum_scr, p_proj[b][:, :], 1.0, ffn_str[dd % 2],
                ALU.mult, ALU.add), inc="vv")
            t_sum = S.emitted["vv"]
            t_ffn_free[dd % 2] = ("vv", t_sum)
            set_bank_free(bank, "vv", t_sum)
            S.wait(TE, "vv", t_sum)
            for tt in range(4):
                bk = tt % 2
                sem2, val2 = pm_ev[bk]
                S.wait(TE, sem2, val2)
                S.op(TE, lambda e, tt=tt, b=bk: e.transpose(
                    p_m[b][:, 128:256], sum_scr[:, 128 * tt:128 * tt + 128], ident[:, :]), inc="mm")
                t_tp = S.emitted["mm"]
                S.wait(A_, "mm", t_tp)
                if dd == 0:
                    S.wait(A_, "Gout", b_t["out_free"])
                S.op(A_, lambda e, tt=tt, dd=dd, b=bk: e.copy(
                    out_tok[tt][:, 128 * dd:128 * dd + 128],
                    p_m[b][:, 128:256]), inc="ss")
                pm_ev[bk] = ("ss", S.emitted["ss"])
        emit_matmuls("wo", ygh[st % 2], ygl[st % 2], t_yg, evac_wo)
        b_t["yg_used"][st % 2] = S.emitted["mm"]
        pump_weights()
        t_all = S.emitted["ss"]
        if _stopped("B_wo"):
            return _finish(nc, S, ctx)
        S.wait(G, "ss", t_all)
        S.wait(G, "Gout", b_t["xpre"])
        for tt in range(4):
            b_t["out_free"] = S.dma(G, out_hbm[TCH * st + 128 * tt: TCH * st + 128 * tt + 128, :],
                                    out_tok[tt], stream="Gout", accum=True)
    finish_weight("wo")
    S.wait(G, "Gout", S.emitted["Gout"])

    return _finish(nc, S, ctx)


def _finish(nc, S, ctx):
    for sname in list(S.emitted):
        S.handles[sname] = ctx.enter_context(nc.semaphore("sem_" + sname))
    with nc.Block() as block:
        @block.sync
        def _(eng):
            for f in S.q["sync"]:
                f(eng)

        @block.tensor
        def _(eng):
            for f in S.q["tensor"]:
                f(eng)

        @block.vector
        def _(eng):
            for f in S.q["vector"]:
                f(eng)

        @block.scalar
        def _(eng):
            for f in S.q["scalar"]:
                f(eng)

        @block.gpsimd
        def _(eng):
            for f in S.q["gpsimd"]:
                f(eng)

    ctx.close()
    return nc


def kernel(**inputs):
    x = np.asarray(inputs["L0_attn_x"], dtype=np.float32).reshape(T, D)
    wmap = {
        "w_ww": inputs["L0_attn_Ww"], "w_wk": inputs["L0_attn_Wk"],
        "w_wv": inputs["L0_attn_Wv"], "w_wr": inputs["L0_attn_Wr"],
        "w_wg": inputs["L0_attn_Wg"], "w_fr": inputs["L0_ffn_W_r"],
        "w_fk": inputs["L0_ffn_W_k"], "w_fv": inputs["L0_ffn_W_v"],
        "w_wo": inputs["L0_attn_Wo"],
    }
    wmap = {k: np.ascontiguousarray(np.asarray(v, dtype=np.float32).T)
            for k, v in wmap.items()}
    vecs = np.ascontiguousarray(np.stack([
        np.asarray(inputs["L0_attn_mix_k"], np.float32).ravel(),
        np.asarray(inputs["L0_attn_mix_v"], np.float32).ravel(),
        np.asarray(inputs["L0_attn_mix_r"], np.float32).ravel(),
        np.asarray(inputs["L0_attn_mix_w"], np.float32).ravel(),
        np.asarray(inputs["L0_attn_time_decay"], np.float32).ravel(),
        np.asarray(inputs["L0_attn_time_first"], np.float32).ravel(),
        np.asarray(inputs["L0_attn_ln_w"], np.float32).ravel(),
        np.asarray(inputs["L0_attn_ln_b"], np.float32).ravel(),
        np.asarray(inputs["L0_ffn_mu_r"], np.float32).ravel(),
        np.asarray(inputs["L0_ffn_mu_k"], np.float32).ravel(),
    ]))
    ident = np.eye(P, dtype=np.float32)
    maska = np.ascontiguousarray(
        np.tile(np.triu(np.ones((P, P), np.float32), 1), (1, 4)))
    onesb = np.zeros((P, 2), np.float32)
    onesb[:64, 0] = 1.0
    onesb[64:, 1] = 1.0
    epsc = np.full((P, 1), EPS, np.float32)

    in_maps = []
    for ci in range(NCORES):
        lead = np.zeros((1, D), np.float32) if ci == 0 else x[Tc * ci - 1: Tc * ci]
        xs = np.ascontiguousarray(np.concatenate([lead, x[Tc * ci: Tc * (ci + 1)]], 0))
        cmask = np.zeros((P, 8), np.float32)
        cmask[:, :ci] = 1.0
        pmask = np.full((P, 1), 0.0 if ci == 0 else 1.0, np.float32)
        m = {"x": xs, "vecs": vecs, "ident": ident, "maska": maska,
             "onesb": onesb, "cmask": cmask, "pmask": pmask, "epsc": epsc}
        m.update(wmap)
        in_maps.append(m)

    nc = build_program()
    res = run_bass_kernel_spmd(nc, in_maps, core_ids=list(range(NCORES)))
    out = np.concatenate([res.results[ci]["out"] for ci in range(NCORES)], axis=0)
    return np.ascontiguousarray(out.reshape(1, T, D).astype(np.float32))


# version 27 (final)
# speedup vs baseline: 1.0131x; 1.0131x over previous
import contextlib
import numpy as np
import concourse.bass as bass
import concourse.mybir as mybir
from concourse.bass_utils import run_bass_kernel_spmd

F32 = mybir.dt.float32
BF16 = mybir.dt.bfloat16
AF = mybir.ActivationFunctionType
ALU = mybir.AluOpType
AX = mybir.AxisListType

NCORES = 8
D = 1024
T = 16384
Tc = 2048
P = 128
NP = 8
L = 128
NCH = 16
TCH = 512
NST = 4
H = 16
NH = 64
EPS = 1e-5
CLAMP = 75.0

ENGINES = ("sync", "tensor", "vector", "scalar", "gpsimd")
SEMS = ("mm", "vv", "ss", "cc", "gg")
V, A_, G, SY, TE = "vector", "scalar", "gpsimd", "sync", "tensor"


class Sched:
    def __init__(self):
        self.q = {e: [] for e in ENGINES}
        self.emitted = {s: 0 for s in SEMS}
        self.handles = {}
        self.bg = []
        self.bg_hi = []
        self.bar_snap = {}
        self.barscr = None

    def sem(self, name):
        if name not in self.emitted:
            self.emitted[name] = 0
        return name

    def wait(self, eng, sem, val):
        assert val <= self.emitted[sem], (eng, sem, val, self.emitted[sem])
        if val <= 0:
            return
        self.q[eng].append(lambda e, s=sem, v=val: e.wait_ge(self.handles[s], v))

    def op(self, eng, fn, inc=None, n=1):
        if inc is not None:
            self.emitted[inc] += n
        def run(e, fn=fn, inc=inc, n=n):
            r = fn(e)
            if inc is not None:
                r.then_inc(self.handles[inc], n)
        self.q[eng].append(run)
        return self.emitted[inc] if inc is not None else None

    def drain(self, eng):
        self.q[eng].append(lambda e: e.drain())

    def pump_bg(self, n=1):
        for i in range(n):
            if self.bg_hi and i == 0:
                self.bg_hi.pop(0)()
            elif self.bg:
                self.bg.pop(0)()
            elif self.bg_hi:
                self.bg_hi.pop(0)()

    def dma(self, eng, out_ap, in_ap, stream="Lmisc", slow=False, accum=False):
        sem = self.sem(stream)
        if accum:
            return self.op(eng, lambda e, o=out_ap, i=in_ap: e.dma_start(
                out=o, in_=i, accum_op=ALU.add), inc=sem, n=16)
        if slow:
            return self.op(eng, lambda e, o=out_ap, i=in_ap: e.dma_start(
                out=o, in_=i, allow_slow_non_contiguous=True), inc=sem, n=16)
        return self.op(eng, lambda e, o=out_ap, i=in_ap: e.dma_start(out=o, in_=i),
                       inc=sem, n=16)

    def barrier(self, skip=("cc",)):
        snap = dict(self.emitted)
        for sem, v in snap.items():
            if sem in skip or v <= 0 or sem == "vv":
                continue
            if self.bar_snap.get(sem, 0) == v:
                continue
            self.wait(V, sem, v)
        self.op(V, lambda e: e.memset(self.barscr[:, :], 0.0), inc="vv")
        t = self.emitted["vv"]
        self.bar_snap = dict(self.emitted)
        for eng in ENGINES:
            if eng == V:
                continue
            self.wait(eng, "vv", t)


def build_program(stop=None):
    nc = bass.Bass("TRN2", target_bir_lowering=False)
    S = Sched()
    def _stopped(name):
        return stop is not None and name == stop

    x_in = nc.dram_tensor("x", [Tc + 1, D], F32, kind="ExternalInput").ap()
    wts = {}
    for wn in ["ww", "wk", "wv", "wr", "wg", "fr", "fk", "fv", "wo"]:
        wts[wn] = nc.dram_tensor("w_" + wn, [D, D], F32, kind="ExternalInput").ap()
    vecs = nc.dram_tensor("vecs", [10, D], F32, kind="ExternalInput").ap()
    ident_in = nc.dram_tensor("ident", [P, P], F32, kind="ExternalInput").ap()
    maska_in = nc.dram_tensor("maska", [P, 4 * P], F32, kind="ExternalInput").ap()
    onesb_in = nc.dram_tensor("onesb", [P, 2], F32, kind="ExternalInput").ap()
    cmask_in = nc.dram_tensor("cmask", [P, 8], F32, kind="ExternalInput").ap()
    pmask_in = nc.dram_tensor("pmask", [P, 1], F32, kind="ExternalInput").ap()
    eps_in = nc.dram_tensor("epsc", [P, 1], F32, kind="ExternalInput").ap()
    out_hbm = nc.dram_tensor("out", [Tc, D], F32, kind="ExternalOutput").ap()

    s_xnT = nc.dram_tensor("s_xnT", [16, P, D], F32)
    s_row = nc.dram_tensor("s_row", [2, D], F32)
    spill = {}
    for n in ["pexp", "k", "r", "g", "rt", "khat", "rf", "ffn"]:
        spill[n] = nc.dram_tensor("s_" + n, [NP, P, Tc], F32)
    s_vt = nc.dram_tensor("s_vt", [NCH, P, D], F32)
    s_sloc = nc.dram_tensor("s_sloc", [NCH, P, D], F32)
    cc_in = nc.dram_tensor("cc_in", [P, D + NP], F32)
    cc_out = nc.dram_tensor("cc_out", [NCORES, P, D + NP], F32)

    ctx = contextlib.ExitStack()
    sb = lambda name, shape: ctx.enter_context(nc.sbuf_tensor(name, shape, F32))
    sbB = lambda name, shape: ctx.enter_context(nc.sbuf_tensor(name, shape, BF16))

    wbh = [sbB("wbh0", [P, NP * D]), sbB("wbh1", [P, NP * D])]
    wbl = [sbB("wbl0", [P, NP * D]), sbB("wbl1", [P, NP * D])]
    wstg = sb("wstg", [P, D])
    mxh = [sbB("mxh0", [P, 4096]), sbB("mxh1", [P, 4096])]
    mxl = [sbB("mxl0", [P, 4096]), sbB("mxl1", [P, 4096])]
    bigA = sb("bigA", [P, NP * (Tc + 1)])
    bigC = sb("bigC", [P, 6656])
    xtok = [bigC[:, 2048:3072], bigC[:, 3072:4096]]
    xn_tok = bigC[:, 4096:5120]
    xn2_tok = bigC[:, 5120:6144]
    s_pp = [sb("spp0", [P, D]), sb("spp1", [P, D])]
    s_inb = sb("s_inb", [P, D])
    dvals_sb = sb("dvals", [P, NCH * H])
    cend_sb = sb("cend", [P, NP * NCH])
    ident = sb("identt", [P, P])
    maska = sbB("maskaa", [P, 4 * P])
    onesb = sb("onesbb", [P, 2])
    cmask = sb("cmaskk", [P, 8])
    pmask = sb("pmaskk", [P, 1])
    epsc = sb("epscc", [P, 1])
    cols = sb("colsb", [P, 160])
    rawmix = dvals_sb[:, 0:4 * NP]

    dtilc = None
    dfc = sb("dfc", [P, NCH])
    statc = dfc
    S.barscr = dfc[0:1, 0:1]
    dtilc = dfc[:, 0:8]

    lmaskc = dfc[:, 8:16]
    plocc = lmaskc

    p_proj = [ctx.enter_context(nc.psum_tensor("pproj0", [P, 512], F32)),
              ctx.enter_context(nc.psum_tensor("pproj1", [P, 512], F32))]
    p_a = [ctx.enter_context(nc.psum_tensor("pa0", [P, 512], F32)),
           ctx.enter_context(nc.psum_tensor("pa1", [P, 512], F32))]
    p_yt = ctx.enter_context(nc.psum_tensor("pyt", [P, 1024], F32))
    p_m = [ctx.enter_context(nc.psum_tensor("pm0", [P, 512], F32)),
           ctx.enter_context(nc.psum_tensor("pm1", [P, 512], F32))]

    for _s in ["Lc", "Lx0", "Lx1", "Gstg", "Gmisc", "Lw0", "Lw1", "Gsp",
               "Lp", "Lk2", "Lr2", "Grt", "Gkh", "Lvt0", "Lvt1", "Gsl", "Gcc",
               "Lc2", "Gffn", "Lrf0", "Lrf1", "Lgb0", "Lgb1", "Lg", "Lkh",
               "Lrt", "Lvtb", "Lslb", "Lfn0", "Lfn1", "Lxtok", "Gout"]:
        S.sem(_s)

    t_ident = S.dma(SY, ident[:, :], ident_in, stream="Lc")
    S.dma(G, maska[:, :], maska_in, stream="Lmk")
    S.dma(SY, onesb[:, :], onesb_in, stream="Lc")
    S.dma(SY, cmask[:, :], cmask_in, stream="Lc")
    S.dma(SY, pmask[:, :], pmask_in, stream="Lc")
    S.dma(SY, epsc[:, :], eps_in, stream="Lc")
    for i in range(4):
        S.dma(SY, rawmix[:, i * NP:(i + 1) * NP],
              vecs[i].rearrange("(p j) -> j p", j=P), stream="Lc", slow=True)
    for i, co in [(4, 64), (5, 72), (6, 80), (7, 88), (8, 96), (9, 104)]:
        S.dma(SY, cols[:, co:co + NP], vecs[i].rearrange("(p j) -> j p", j=P), stream="Lc", slow=True)
    t_consts = S.emitted["Lc"]
    S.wait(A_, "Lc", t_consts)
    S.op(A_, lambda e: e.activation(cols[:, 0:32], rawmix[:, 0:32], AF.Sigmoid), inc="ss")
    t_sig = S.emitted["ss"]
    S.wait(V, "ss", t_sig)
    S.wait(V, "Lc", t_consts)
    S.op(V, lambda e: e.tensor_scalar(cols[:, 32:64], cols[:, 0:32], -1.0, 1.0,
                                      ALU.mult, ALU.add), inc="vv")
    S.op(V, lambda e: e.tensor_scalar(cols[:, 128:144], cols[:, 96:112], -1.0, 1.0,
                                      ALU.mult, ALU.add), inc="vv")
    S.op(V, lambda e: e.memset(cols[:, 144:160], 0.0), inc="vv")
    S.barrier()

    if _stopped("P0"):
        return _finish(nc, S, ctx)

    t1scr = bigC[:, 0:512]
    mixscr = [bigC[:, 512:1024], bigC[:, 1024:1536]]
    mixscr2 = [s_inb[:, 0:512], s_inb[:, 512:1024]]
    scrE = bigC[:, 1536:3584]
    vscr = bigC[:, 3584:4096]
    relscr = bigC[:, 4096:4608]
    sqf = bigC[:, 4608:5120]
    fb = [bigC[:, 5120:5632], bigC[:, 5632:6144]]
    vtb = [s_pp[1][:, 0:512], s_pp[1][:, 512:1024]]
    rfstr = [vscr, bigC[:, 6144:6656]]
    mx_used = [0, 0]
    w_ready = {}
    w_free_t = [0, 0]
    w_slot = {}
    wstg_last = [(0, 0), (0, 0)]
    _order = ["ww", "wk", "wv", "wr", "wg", "fr", "fk", "fv", "wo"]
    S.sem("Lws0")
    S.sem("Lws1")
    S.sem("Lwh0")
    S.sem("Lwh1")

    def load_weight(i):
        wn = _order[i]
        slot = i % 2
        w_slot[wn] = slot
        w_ready[wn] = [None, None]
        free_mm = w_free_t[slot]

        def mk(q):
            def c():
                if q == 0:
                    S.wait(SY, "mm", free_mm)
                ws, wv = wstg_last[q % 2]
                S.wait(SY, "ss", ws)
                S.wait(SY, "vv", wv)
                src = wts[wn].rearrange("(kk j) d -> j kk d", j=P)[
                    :, q // 2, 512 * (q % 2):512 * (q % 2) + 512]
                half = wstg[:, 512 * (q % 2):512 * (q % 2) + 512]
                t_l = S.dma(SY, half, src, stream="Lws%d" % (q % 2))
                S.wait(A_, "Lws%d" % (q % 2), t_l)
                if q == 0:
                    S.wait(A_, "mm", free_mm)
                S.op(A_, lambda e: e.copy(
                    wbh[slot][:, 512 * q:512 * q + 512], half), inc="ss")
                th = S.emitted["ss"]
                S.wait(V, "ss", th)
                if q == 0:
                    S.wait(V, "mm", free_mm)
                S.op(V, lambda e: e.tensor_tensor(
                    wbl[slot][:, 512 * q:512 * q + 512], half,
                    wbh[slot][:, 512 * q:512 * q + 512], ALU.subtract), inc="vv")
                wstg_last[q % 2] = (th, S.emitted["vv"])
                if q == 15:
                    w_ready[wn][0] = th
                    w_ready[wn][1] = S.emitted["vv"]
            return c
        for q in range(16):
            S.bg.append(mk(q))

    wi_next = [0]
    wi_pending = [2]

    def finish_weight(wn):
        w_free_t[w_slot[wn]] = S.emitted["mm"]
        wi_pending[0] += 1

    def pump_weights():
        while wi_pending[0] > 0 and wi_next[0] < len(_order):
            load_weight(wi_next[0])
            wi_next[0] += 1
            wi_pending[0] -= 1
    pump_weights()

    mixscr_last = [[(0, 0), (0, 0)], [(0, 0), (0, 0)]]
    mixscr_sets = [mixscr, mixscr2]

    def emit_mix_bg(st, sig_off, oms_off, src_fn, buf, guard_mm):
        holder = [None]
        t1p = [t1scr, relscr]
        ms = mixscr_sets[buf]

        def mk(pp):
            def c():
                if pp == 0:
                    S.wait(V, "mm", guard_mm[0])
                    S.wait(A_, "mm", guard_mm[0])
                for j in (0, 1):
                    p = pp + j
                    cur = src_fn(p)[:, 1 + TCH * st: 1 + TCH * st + TCH]
                    ps, pv = mixscr_last[buf][j]
                    S.wait(V, "ss", ps)
                    S.wait(V, "vv", pv)
                    S.op(V, lambda e, cur=cur, p=p, j=j, so=sig_off: e.tensor_scalar(
                        t1p[j][:, :], cur, cols[:, so + p:so + p + 1], None,
                        ALU.mult, ALU.bypass), inc="vv")
                S.drain(V)
                for j in (0, 1):
                    p = pp + j
                    shf = src_fn(p)[:, TCH * st: TCH * st + TCH]
                    S.op(V, lambda e, shf=shf, p=p, j=j, oo=oms_off: e.scalar_tensor_tensor(
                        ms[j], shf, cols[:, oo + p:oo + p + 1],
                        t1p[j][:, :], ALU.mult, ALU.add), inc="vv")
                S.drain(V)
                t_m = S.emitted["vv"]
                S.wait(A_, "vv", t_m)
                for j in (0, 1):
                    p = pp + j
                    S.op(A_, lambda e, p=p, j=j, buf=buf: e.copy(
                        mxh[buf][:, TCH * p:TCH * p + TCH], ms[j]), inc="ss")
                t_h = S.emitted["ss"]
                S.wait(V, "ss", t_h)
                for j in (0, 1):
                    p = pp + j
                    S.op(V, lambda e, p=p, j=j, buf=buf: e.tensor_tensor(
                        mxl[buf][:, TCH * p:TCH * p + TCH], ms[j],
                        mxh[buf][:, TCH * p:TCH * p + TCH], ALU.subtract), inc="vv")
                S.drain(V)
                for j in (0, 1):
                    mixscr_last[buf][j] = (t_h, S.emitted["vv"])
                if pp == NP - 2:
                    holder[0] = (S.emitted["ss"], S.emitted["vv"])
            return c
        for pp in range(0, NP, 2):
            S.bg_hi.append(mk(pp))
        return holder

    def emit_mix(st, sig_off, oms_off, src_fn):
        buf = st % 2
        h = emit_mix_bg(st, sig_off, oms_off, src_fn,
                        buf, [max(mx_used[buf], 0)])
        while h[0] is None:
            S.pump_bg(1)
        return h[0]

    proj_bank_free = [("vv", 0), ("vv", 0)]

    def emit_matmuls(wn, rhs_hi, rhs_lo, t_rhs, evac_cb):
        if wn not in w_slot:
            pump_weights()
        while w_ready[wn][1] is None:
            S.pump_bg(4)
        if isinstance(t_rhs, list):
            while t_rhs[0] is None:
                S.pump_bg(1)
            t_rhs = t_rhs[0]
        slot = w_slot[wn]
        S.pump_bg(1)
        S.pump_bg(1)
        S.pump_bg(1)
        S.wait(TE, "ss", max(w_ready[wn][0], t_rhs[0]))
        S.wait(TE, "vv", max(w_ready[wn][1], t_rhs[1]))
        for dd in range(NP):
            bank = dd % 2
            sem, val = proj_bank_free[bank]
            S.wait(TE, sem, val)
            for kk in range(NP):
                wsl = slice(kk * D + dd * P, kk * D + dd * P + P)
                rsl = slice(TCH * kk, TCH * kk + TCH)
                S.op(TE, lambda e, b=bank, slot=slot, wsl=wsl, rsl=rsl, kk=kk:
                     e.matmul(p_proj[b][:, :], wbh[slot][:, wsl], rhs_hi[:, rsl],
                              start=(kk == 0), stop=False), inc="mm")
                S.op(TE, lambda e, b=bank, slot=slot, wsl=wsl, rsl=rsl:
                     e.matmul(p_proj[b][:, :], wbh[slot][:, wsl], rhs_lo[:, rsl],
                              start=False, stop=False), inc="mm")
                S.op(TE, lambda e, b=bank, slot=slot, wsl=wsl, rsl=rsl, kk=kk:
                     e.matmul(p_proj[b][:, :], wbl[slot][:, wsl], rhs_hi[:, rsl],
                              start=False, stop=(kk == NP - 1)), inc="mm")
            evac_cb(dd, bank, S.emitted["mm"])
            S.pump_bg(4)

    def set_bank_free(bank, sem, val):
        proj_bank_free[bank] = (sem, val)

    scrE_t = [0]

    def _set_scrE(v):
        scrE_t[0] = v

    def spill_batch(dram, st, dd):
        dst = dram.ap()[dd - 3:dd + 1].rearrange(
            "p i t -> i p t")[:, :, TCH * st:TCH * st + TCH]
        return S.dma(G, dst, scrE.rearrange("i (p t) -> i p t", t=TCH),
                     stream="Gsp")

    xn2v = lambda p: bigA[:, p * (Tc + 1):(p + 1) * (Tc + 1)]
    stg = bigC[:, 0:1024]
    sqscr = bigC[:, 1024:2048]
    pm_ev = [("vv", 0), ("vv", 0)]
    t_stg_free = 0
    t_xt_used = [("vv", 0), ("vv", 0)]
    t_tile_trans = 0
    t_apply0 = 0
    for j in range(17):
        S.pump_bg(2)
        w = 128 if j < 16 else 1
        xt = xtok[j % 2]
        sem, val = t_xt_used[j % 2]
        S.wait(SY, sem, val)
        t_ld = S.dma(SY, xt[0:w, :], x_in[128 * j:128 * j + w, :],
                     stream="Lx%d" % (j % 2))
        for ln in range(2):
            src = xt[0:w, :] if ln == 0 else xn_tok[0:w, :]
            o = 7 * ln
            if ln == 0:
                S.wait(V, "Lx%d" % (j % 2), t_ld)
                S.wait(A_, "Lx%d" % (j % 2), t_ld)
            else:
                S.wait(A_, "vv", t_apply0)
            S.op(V, lambda e, s=src, o=o, w=w: e.tensor_reduce(
                statc[0:w, o:o + 1], s, AX.X, ALU.add), inc="vv")
            S.drain(V)
            S.op(A_, lambda e, s=src, o=o, w=w: e.activation(
                sqscr[0:w, :], s, AF.Square, accum_out=statc[0:w, o + 1:o + 2]), inc="ss")
            t_acc = S.emitted["ss"]
            S.op(V, lambda e, o=o, w=w: e.tensor_scalar(
                statc[0:w, o + 2:o + 3], statc[0:w, o:o + 1], 1.0 / D, None,
                ALU.mult, ALU.bypass), inc="vv")
            S.drain(V)
            S.op(V, lambda e, o=o, w=w: e.tensor_tensor(
                statc[0:w, o + 3:o + 4], statc[0:w, o + 2:o + 3],
                statc[0:w, o + 2:o + 3], ALU.mult), inc="vv")
            S.drain(V)
            S.op(V, lambda e, o=o, w=w: e.tensor_scalar(
                statc[0:w, o + 4:o + 5], statc[0:w, o + 3:o + 4], -1.0, EPS,
                ALU.mult, ALU.add), inc="vv")
            t_pre = S.emitted["vv"]
            S.wait(A_, "vv", t_pre)
            S.drain(A_)
            S.op(A_, lambda e, o=o, w=w: e.activation(
                statc[0:w, o + 5:o + 6], statc[0:w, o + 1:o + 2], AF.Sqrt,
                bias=statc[0:w, o + 4:o + 5], scale=1.0 / D), inc="ss")
            t_std = S.emitted["ss"]
            S.wait(V, "ss", t_std)
            S.op(V, lambda e, o=o, w=w: e.reciprocal(
                statc[0:w, o + 6:o + 7], statc[0:w, o + 5:o + 6]), inc="vv")
            S.drain(V)
            dst = xn_tok if ln == 0 else xn2_tok
            S.wait(V, "mm", t_tile_trans)
            S.op(V, lambda e, s=src, d=dst, o=o, w=w: e.tensor_scalar(
                d[0:w, :], s, statc[0:w, o + 2:o + 3], statc[0:w, o + 6:o + 7],
                ALU.subtract, ALU.mult), inc="vv")
            if ln == 0:
                t_apply0 = S.emitted["vv"]
                S.drain(V)
                t_xt_used[j % 2] = ("vv", S.emitted["vv"])
        t_apply = S.emitted["vv"]
        if j < 16:
            S.wait(TE, "vv", t_apply)
            S.wait(TE, "Lc", t_consts)
            for half in range(2):
                for src_i, (srcT, bank) in enumerate([(xn_tok, 0), (xn2_tok, 1)]):
                    sem, val = pm_ev[bank]
                    S.wait(TE, sem, val)
                    for q in range(4):
                        pp_ = 4 * half + q
                        S.op(TE, lambda e, s=srcT, q=q, pp_=pp_, b=bank: e.transpose(
                            p_m[b][:, 128 * q:128 * q + 128],
                            s[:, 128 * pp_:128 * pp_ + 128], ident[:, :]), inc="mm")
                    t_tp = S.emitted["mm"]
                    if src_i == 0:
                        S.wait(A_, "mm", t_tp)
                        S.wait(A_, "Gstg", t_stg_free)
                        for q in range(4):
                            pp_ = 4 * half + q
                            S.op(A_, lambda e, q=q, pp_=pp_: e.copy(
                                stg[:, 128 * pp_:128 * pp_ + 128],
                                p_m[0][:, 128 * q:128 * q + 128]), inc="ss")
                        pm_ev[0] = ("ss", S.emitted["ss"])
                    else:
                        S.wait(V, "mm", t_tp)
                        for q in range(4):
                            pp_ = 4 * half + q
                            S.op(V, lambda e, q=q, pp_=pp_, j=j: e.tensor_scalar(
                                xn2v(pp_)[:, 128 * j:128 * j + 128],
                                p_m[1][:, 128 * q:128 * q + 128],
                                cols[:, 80 + pp_:81 + pp_], cols[:, 88 + pp_:89 + pp_],
                                ALU.mult, ALU.add), inc="vv")
                        pm_ev[1] = ("vv", S.emitted["vv"])
            t_tile_trans = S.emitted["mm"]
            S.wait(G, "ss", pm_ev[0][1])
            t_stg_free = S.dma(G, s_xnT.ap()[j], stg, stream="Gstg")
        else:
            S.wait(G, "vv", t_apply)
            S.dma(G, s_row.ap()[0:1, :], xn_tok[0:1, :], stream="Gmisc")
            t_rw = S.dma(G, s_row.ap()[1:2, :], xn2_tok[0:1, :], stream="Gmisc")
            S.wait(G, "Gmisc", t_rw)
            for pp_ in range(NP):
                S.dma(G, xn2v(pp_)[:, Tc:Tc + 1],
                      s_row.ap()[1].rearrange("(p j) -> j p", j=P)[:, pp_:pp_ + 1],
                      stream="Gmisc", slow=True)
            t_sm = S.emitted["Gmisc"]
            S.wait(V, "Gmisc", t_sm)
            for pp_ in range(NP):
                S.op(V, lambda e, pp_=pp_: e.tensor_scalar(
                    xn2v(pp_)[:, Tc:Tc + 1], xn2v(pp_)[:, Tc:Tc + 1],
                    cols[:, 80 + pp_:81 + pp_], cols[:, 88 + pp_:89 + pp_],
                    ALU.mult, ALU.add), inc="vv")
    S.drain(V)
    for pp_ in range(NP):
        S.op(V, lambda e, pp_=pp_: e.tensor_scalar(
            xn2v(pp_)[:, 0:1], xn2v(pp_)[:, 0:1], pmask[:, 0:1], None,
            ALU.mult, ALU.bypass), inc="vv")
    S.barrier()

    if _stopped("S1"):
        return _finish(nc, S, ctx)
    def mk_evac_w(st):
        def evac_w(dd, bank, t_mm):
            S.wait(V, "mm", t_mm)
            if dd % 4 == 0:
                S.wait(V, "Gsp", scrE_t[0])
            sl = scrE[:, 512 * (dd % 4):512 * (dd % 4) + 512]
            S.op(V, lambda e, b=bank, sl=sl, dd=dd: e.tensor_scalar(
                sl, p_proj[b][:, :], cols[:, 64 + dd:65 + dd], 10.0,
                ALU.add, ALU.min), inc="vv")
            S.drain(V)
            S.op(V, lambda e, sl=sl: e.tensor_scalar(
                sl, sl, -10.0, None, ALU.max, ALU.bypass), inc="vv")
            S.drain(V)
            set_bank_free(bank, "vv", S.emitted["vv"])
            if dd % 4 == 3:
                t_v = S.emitted["vv"]
                S.wait(A_, "vv", t_v)
                S.op(A_, lambda e: e.activation(scrE[:, :], scrE[:, :], AF.Exp), inc="ss")
                t_e = S.emitted["ss"]
                S.wait(G, "ss", t_e)
                _set_scrE(spill_batch(spill["pexp"], st, dd))
        return evac_w

    def mk_evac_k(st):
        def evac_k(dd, bank, t_mm):
            S.wait(A_, "mm", t_mm)
            if dd % 4 == 0:
                S.wait(A_, "Gsp", scrE_t[0])
            S.op(A_, lambda e, b=bank, dd=dd: e.copy(
                scrE[:, 512 * (dd % 4):512 * (dd % 4) + 512], p_proj[b][:, :]), inc="ss")
            set_bank_free(bank, "ss", S.emitted["ss"])
            if dd % 4 == 3:
                S.wait(G, "ss", S.emitted["ss"])
                _set_scrE(spill_batch(spill["k"], st, dd))
        return evac_k

    t_vtb = [0, 0]
    S.sem("Gvt0")
    S.sem("Gvt1")

    def mk_evac_v(st):
        def evac_v(dd, bank, t_mm):
            S.wait(A_, "mm", t_mm)
            S.op(A_, lambda e, b=bank: e.copy(vscr, p_proj[b][:, :]), inc="ss")
            t_vs = S.emitted["ss"]
            set_bank_free(bank, "ss", t_vs)
            S.wait(TE, "ss", t_vs)
            sem, val = pm_ev[0]
            S.wait(TE, sem, val)
            for tt in range(4):
                S.op(TE, lambda e, tt=tt: e.transpose(
                    p_m[0][:, 128 * tt:128 * tt + 128],
                    vscr[:, 128 * tt:128 * tt + 128], ident[:, :]), inc="mm")
            t_tp = S.emitted["mm"]
            S.wait(A_, "mm", t_tp)
            S.wait(A_, "Gvt%d" % (dd % 2), t_vtb[dd % 2])
            for tt in range(4):
                S.op(A_, lambda e, tt=tt, dd=dd: e.copy(
                    vtb[dd % 2][:, 128 * tt:128 * tt + 128],
                    p_m[0][:, 128 * tt:128 * tt + 128]), inc="ss")
            pm_ev[0] = ("ss", S.emitted["ss"])
            S.wait(G, "ss", S.emitted["ss"])
            t_vtb[dd % 2] = S.dma(
                G,
                s_vt.ap()[4 * st:4 * st + 4][:, :, 128 * dd:128 * dd + 128].rearrange("c i q -> i c q"),
                vtb[dd % 2].rearrange("i (c q) -> i c q", q=128),
                stream="Gvt%d" % (dd % 2))
        return evac_v

    def mk_evac_sig(st, dest):
        def evac_sig(dd, bank, t_mm):
            S.wait(A_, "mm", t_mm)
            if dd % 4 == 0:
                S.wait(A_, "Gsp", scrE_t[0])
            S.op(A_, lambda e, b=bank, dd=dd: e.activation(
                scrE[:, 512 * (dd % 4):512 * (dd % 4) + 512],
                p_proj[b][:, :], AF.Sigmoid), inc="ss")
            set_bank_free(bank, "ss", S.emitted["ss"])
            if dd % 4 == 3:
                S.wait(G, "ss", S.emitted["ss"])
                _set_scrE(spill_batch(dest, st, dd))
        return evac_sig

    units = []
    for st in range(NST):
        units.append(((24, 56, xn2v, st), [("ww", mk_evac_w(st))]))
    for st in range(NST):
        units.append(((0, 32, xn2v, st), [("wk", mk_evac_k(st))]))
    for st in range(NST):
        units.append(((8, 40, xn2v, st), [("wv", mk_evac_v(st))]))
    for st in range(NST):
        units.append(((16, 48, xn2v, st),
                      [("wr", mk_evac_sig(st, spill["r"])),
                       ("wg", mk_evac_sig(st, spill["g"]))]))
    fin_after = {3: ["ww"], 7: ["wk"], 11: ["wv"], 15: ["wr", "wg"]}

    mixh = {0: emit_mix_bg(units[0][0][3], units[0][0][0], units[0][0][1],
                           units[0][0][2], 0, [0])}
    for k, (mspec, mmsets) in enumerate(units):
        if k + 1 < len(units):
            sig2, oms2, src2, st2 = units[k + 1][0]
            mixh[k + 1] = emit_mix_bg(st2, sig2, oms2, src2, (k + 1) % 2,
                                      [mx_used[(k + 1) % 2]])
        for wn, ecb in mmsets:
            emit_matmuls(wn, mxh[k % 2], mxl[k % 2], mixh[k], ecb)
        mx_used[k % 2] = S.emitted["mm"]
        for wn in fin_after.get(k, []):
            finish_weight(wn)
        pump_weights()
    S.barrier()

    if _stopped("S7a"):
        return _finish(nc, S, ctx)
    pexp_p = bigA[:, 0:Tc]
    k_p = bigA[:, Tc:2 * Tc]
    r_p = bigA[:, 2 * Tc:3 * Tc]
    C_p = bigA[:, 3 * Tc:4 * Tc]
    sA = bigA[:, 4 * Tc:5 * Tc]
    sB = bigA[:, 5 * Tc:6 * Tc]
    sU = bigA[:, 6 * Tc:7 * Tc]
    vt_all = bigA[:, 7 * Tc:8 * Tc]
    ktilT_all = sA
    shist = sB
    S.op(V, lambda e: e.memset(s_pp[0][:, :], 0.0), inc="vv")
    S.drain(V)
    t_spp0_ms = S.emitted["vv"]
    t_df_used = 0
    t_sl_dma = 0
    t_u2_done = 0
    pa_free = [("vv", 0), ("vv", 0)]
    t_pexp_free = 0
    t_k_free = 0
    t_r_free = 0
    t_cp_free = 0
    for p in range(NP):
        S.wait(SY, "vv", t_pexp_free)
        t_lp = S.dma(SY, pexp_p, spill["pexp"].ap()[p], stream="Lp")
        S.wait(SY, "vv", t_k_free)
        t_lk = S.dma(SY, k_p, spill["k"].ap()[p], stream="Lk2")
        S.wait(SY, "vv", t_r_free)
        t_lr = S.dma(SY, r_p, spill["r"].ap()[p], stream="Lr2")
        S.wait(SY, "mm", t_u2_done)
        t_vta = S.dma(SY, vt_all.rearrange("i (c q) -> i c q", q=128),
                      s_vt.ap()[:, :, 128 * p:128 * p + 128].rearrange("c i q -> i c q"),
                      stream="Lvt0")
        S.wait(V, "Lp", t_lp)
        S.wait(V, "vv", t_cp_free)
        for c in range(NCH):
            S.op(V, lambda e, c=c: e.tensor_tensor_scan(
                C_p[:, L * c:L * c + L], pexp_p[:, L * c:L * c + L],
                pexp_p[:, L * c:L * c + L], 0.0, ALU.add, ALU.bypass), inc="vv")
        S.drain(V)
        S.op(V, lambda e, p=p: e.tensor_copy(
            cend_sb[:, NCH * p:NCH * p + NCH], C_p[:, L - 1::L]), inc="vv")
        S.drain(V)
        S.op(V, lambda e, p=p: e.tensor_reduce(
            cols[:, 144 + p:145 + p], cend_sb[:, NCH * p:NCH * p + NCH],
            AX.X, ALU.add), inc="vv")
        S.op(V, lambda e: e.tensor_tensor(sA, pexp_p, C_p, ALU.subtract), inc="vv")
        t_v1 = S.emitted["vv"]
        t_pexp_free = t_v1
        S.wait(A_, "vv", t_v1)
        S.wait(A_, "Gsl", t_sl_dma)
        S.op(A_, lambda e: e.activation(sB, sA, AF.Exp), inc="ss")
        t_s1 = S.emitted["ss"]
        S.wait(V, "ss", t_s1)
        S.wait(V, "Lr2", t_lr)
        S.op(V, lambda e: e.tensor_tensor(sA, r_p, sB, ALU.mult), inc="vv")
        t_rt = S.emitted["vv"]
        S.wait(G, "vv", t_rt)
        t_rt_dma = S.dma(G, spill["rt"].ap()[p], sA, stream="Grt")
        S.drain(V)
        S.op(V, lambda e: e.tensor_scalar(sB, C_p, CLAMP, None, ALU.min, ALU.bypass), inc="vv")
        t_v2 = S.emitted["vv"]
        S.wait(A_, "vv", t_v2)
        S.op(A_, lambda e: e.activation(sU, sB, AF.Exp), inc="ss")
        t_s2 = S.emitted["ss"]
        S.wait(V, "ss", t_s2)
        S.wait(V, "Lk2", t_lk)
        S.op(V, lambda e: e.tensor_tensor(sB, k_p, sU, ALU.mult), inc="vv")
        t_kh = S.emitted["vv"]
        S.wait(G, "vv", t_kh)
        t_kh_dma = S.dma(G, spill["khat"].ap()[p], sB, stream="Gkh")
        S.drain(V)
        S.op(V, lambda e, p=p: e.scalar_tensor_tensor(
            sU, k_p, cols[:, 72 + p:73 + p], r_p, ALU.mult, ALU.mult), inc="vv")
        t_tmp = S.emitted["vv"]
        t_r_free = t_tmp
        S.wait(TE, "vv", t_tmp)
        sem, val = pm_ev[1]
        S.wait(TE, sem, val)
        for c in range(NCH):
            S.op(TE, lambda e, c=c: e.matmul(
                p_m[1][:, 2 * c:2 * c + 2], sU[:, L * c:L * c + L], onesb[:, :],
                start=True, stop=True), inc="mm")
        t_dv = S.emitted["mm"]
        S.wait(V, "mm", t_dv)
        S.op(V, lambda e, p=p: e.tensor_copy(
            dvals_sb[:, :].rearrange("i (c h) -> i c h", h=H)[:, :, 2 * p:2 * p + 2],
            p_m[1][:, 0:32].rearrange("i (c h) -> i c h", h=2)), inc="vv")
        S.drain(V)
        t_dvev = S.emitted["vv"]
        pm_ev[1] = ("vv", t_dvev)
        S.wait(V, "Grt", t_rt_dma)
        for c in range(NCH):
            S.op(V, lambda e, c=c, p=p: e.tensor_scalar(
                sA[:, L * c:L * c + L], C_p[:, L * c:L * c + L],
                cend_sb[:, NCH * p + c:NCH * p + c + 1], None,
                ALU.subtract, ALU.bypass), inc="vv")
        t_v3 = S.emitted["vv"]
        t_cp_free = t_v3
        S.wait(A_, "vv", t_v3)
        S.wait(A_, "Gkh", t_kh_dma)
        S.op(A_, lambda e: e.activation(sB, sA, AF.Exp), inc="ss")
        t_s3 = S.emitted["ss"]
        S.wait(V, "ss", t_s3)
        S.wait(V, "mm", t_dv)
        S.drain(V)
        S.op(V, lambda e: e.tensor_tensor(sU, k_p, sB, ALU.mult), inc="vv")
        t_ktil = S.emitted["vv"]
        t_k_free = t_ktil
        S.op(A_, lambda e, p=p: e.activation(
            dfc[:, :], cend_sb[:, NCH * p:NCH * p + NCH], AF.Exp, scale=-1.0), inc="ss")
        t_dfc = S.emitted["ss"]

        S.wait(V, "mm", S.emitted["mm"])
        S.drain(V)
        S.op(V, lambda e: e.memset(shist[:, 0:128], 0.0), inc="vv")
        S.drain(V)
        ktt_ss = {}
        for c in range(NCH):
            bank01 = (c // 4) % 2
            if c % 4 == 0:
                sem, val = pm_ev[bank01]
                S.wait(TE, sem, val)
                if c == 0:
                    S.wait(TE, "vv", t_ktil)
            S.op(TE, lambda e, c=c, b=bank01: e.transpose(
                p_m[b][:, 128 * (c % 4):128 * (c % 4) + 128],
                sU[:, L * c:L * c + L], ident[:, :]), inc="mm")
            if c % 4 == 3:
                t_tp = S.emitted["mm"]
                S.wait(A_, "mm", t_tp)
                for c2 in range(c - 3, c + 1):
                    S.op(A_, lambda e, c2=c2, b=bank01: e.copy(
                        ktilT_all[:, 128 * c2:128 * c2 + 128],
                        p_m[b][:, 128 * (c2 % 4):128 * (c2 % 4) + 128]), inc="ss")
                pm_ev[bank01] = ("ss", S.emitted["ss"])
                ktt_ss[c // 4] = S.emitted["ss"]
        S.wait(TE, "Lvt0", t_vta)
        for c in range(NCH):
            bank01 = (c // 4) % 2
            if c % 4 == 0:
                S.wait(TE, "ss", ktt_ss[c // 4])
                sem, val = pa_free[bank01]
                S.wait(TE, sem, val)
            S.op(TE, lambda e, c=c, b=bank01: e.matmul(
                p_a[b][:, 128 * (c % 4):128 * (c % 4) + 128],
                ktilT_all[:, 128 * c:128 * c + 128],
                vt_all[:, 128 * c:128 * c + 128],
                start=(c % 4 == 0), stop=(c % 4 == 3)), inc="mm")
            if c % 4 == 3:
                u2_mm = S.emitted["mm"]
                t_u2_done = u2_mm
                S.wait(V, "mm", u2_mm)
                if c == 3:
                    S.wait(V, "ss", t_dfc)
                for c2 in range(c - 3, c + 1):
                    dst = shist[:, 128 * (c2 + 1):128 * (c2 + 1) + 128] \
                        if c2 < NCH - 1 else s_pp[0][:, 128 * p:128 * p + 128]
                    srcb = shist[:, 128 * c2:128 * c2 + 128]
                    for hh in range(2):
                        b0 = 64 * hh
                        S.op(V, lambda e, c2=c2, b0=b0, dst=dst, srcb=srcb, b=bank01: e.scalar_tensor_tensor(
                            dst[b0:b0 + 64, b0:b0 + 64],
                            srcb[b0:b0 + 64, b0:b0 + 64],
                            dfc[b0:b0 + 64, c2:c2 + 1],
                            p_a[b][b0:b0 + 64, 128 * (c2 % 4) + b0:128 * (c2 % 4) + b0 + 64],
                            ALU.mult, ALU.add), inc="vv")
                    S.drain(V)
                pa_free[bank01] = ("vv", S.emitted["vv"])
        t_df_used = S.emitted["vv"]
        S.wait(G, "vv", t_df_used)
        if p == 0:
            S.wait(G, "vv", t_spp0_ms)
        t_sl_dma = S.dma(G, s_sloc.ap()[:, :, 128 * p:128 * p + 128].rearrange("c i q -> i c q"),
                         shist.rearrange("i (c q) -> i c q", q=128), stream="Gsl")
    S.wait(G, "vv", S.emitted["vv"])
    t_ci1 = S.dma(G, cc_in.ap()[:, 0:D], s_pp[0][:, :], stream="Gcc")
    t_ci2 = S.dma(G, cc_in.ap()[:, D:D + NP], cols[:, 144:152], stream="Gcc")
    S.wait(G, "Gcc", t_ci2)
    S.op(G, lambda e: e.collective_compute(
        "AllGather", ALU.bypass, replica_groups=[list(range(NCORES))],
        ins=[cc_in.ap().opt()], outs=[cc_out.ap().opt()]), inc="cc")
    S.barrier()

    if _stopped("S5"):
        return _finish(nc, S, ctx)
    xnv = lambda p: bigA[:, p * (Tc + 1):(p + 1) * (Tc + 1)]
    for p in range(NP):
        S.dma(SY, xnv(p)[:, 0:Tc].rearrange("i (j q) -> i j q", q=P),
              s_xnT.ap()[:, :, 128 * p:128 * p + 128].rearrange("j i q -> i j q"),
              stream="Lc2")
        S.dma(SY, xnv(p)[:, Tc:Tc + 1],
              s_row.ap()[0].rearrange("(pp j) -> j pp", j=P)[:, p:p + 1],
              stream="Lc2", slow=True)
    S.barrier()
    mix_rf = {0: emit_mix(0, 96, 128, xnv)}
    mix_kf = {}
    for st in range(NST):
        if st + 1 < NST:
            mix_rf[st + 1] = emit_mix_bg(st + 1, 96, 128, xnv, (st + 1) % 2,
                                         [mx_used[(st + 1) % 2]])
        else:
            mix_kf[0] = emit_mix_bg(0, 104, 136, xnv, 0, [mx_used[0]])
        def evac_rf(dd, bank, t_mm):
            S.wait(A_, "mm", t_mm)
            if dd % 4 == 0:
                S.wait(A_, "Gsp", scrE_t[0])
            S.op(A_, lambda e, b=bank, dd=dd: e.activation(
                scrE[:, 512 * (dd % 4):512 * (dd % 4) + 512],
                p_proj[b][:, :], AF.Sigmoid), inc="ss")
            set_bank_free(bank, "ss", S.emitted["ss"])
            if dd % 4 == 3:
                S.wait(G, "ss", S.emitted["ss"])
                _set_scrE(spill_batch(spill["rf"], st, dd))
        emit_matmuls("fr", mxh[st % 2], mxl[st % 2], mix_rf[st], evac_rf)
        mx_used[st % 2] = S.emitted["mm"]
        pump_weights()
    finish_weight("fr")
    S.wait(SY, "Gsp", S.emitted["Gsp"])

    xfst = scrE[:, 0:1024]
    xf_state = [xfst[:, 0:512], xfst[:, 512:1024]]
    xf_llc = [cols[:, 112:120], cols[:, 120:128]]
    smask_ps = p_yt
    S.op(V, lambda e: e.memset(s_pp[0][:, :], 0.0), inc="vv")
    S.op(V, lambda e: e.memset(s_pp[1][:, :], 0.0), inc="vv")
    S.drain(V)
    xf_lg = {}
    xf_free = [("vv", 0), ("vv", 0)]
    xf_done = [0]

    def xf_load(m):
        sem, val = xf_free[0]
        S.wait(SY, sem, val)
        if m == 0:
            S.wait(SY, "cc", 1)
        S.dma(SY, xfst[:, 0:1024], cc_out.ap()[m][:, 0:1024], stream="Lgb0")
        xf_lg[m] = S.dma(SY, xf_llc[m % 2], cc_out.ap()[m][:, 1024:1032],
                         stream="Lgb0")

    def xf_fold(m):
        buf = m % 2
        S.wait(V, "Lgb0", xf_lg[m])
        S.op(V, lambda e, b=buf, m=m: e.tensor_tensor(
            lmaskc[:, :], xf_llc[b],
            cmask[:, m:m + 1].to_broadcast([P, NP]), ALU.mult), inc="vv")
        t_lm = S.emitted["vv"]
        S.wait(A_, "vv", t_lm)
        S.op(A_, lambda e: e.activation(dtilc[:, :], lmaskc[:, :], AF.Exp, scale=-1.0), inc="ss")
        t_dt = S.emitted["ss"]
        for hf in range(2):
            S.op(V, lambda e, m=m, hf=hf: e.tensor_tensor(
                smask_ps[:, 512 * hf:512 * hf + 512], xf_state[hf][:, :],
                cmask[:, m:m + 1].to_broadcast([P, 512]), ALU.mult), inc="vv")
        S.drain(V)
        xf_free[0] = ("vv", S.emitted["vv"])
        if m + 1 < NCORES:
            xf_load(m + 1)
        S.wait(V, "ss", t_dt)
        for p in range(NP):
            for hh in range(2):
                b0 = 64 * hh
                S.op(V, lambda e, m=m, p=p, b0=b0: e.scalar_tensor_tensor(
                    s_pp[(m + 1) % 2][b0:b0 + 64, 128 * p + b0:128 * p + b0 + 64],
                    s_pp[m % 2][b0:b0 + 64, 128 * p + b0:128 * p + b0 + 64],
                    dtilc[b0:b0 + 64, p:p + 1],
                    smask_ps[b0:b0 + 64, 128 * p + b0:128 * p + b0 + 64],
                    ALU.mult, ALU.add), inc="vv")
        S.drain(V)
        if m == NCORES - 1:
            xf_done[0] = S.emitted["vv"]

    khat_bd = [bigA[:, 0:2048], bigA[:, 2048:4096]]
    rt_b = [bigA[:, 4096:5120], bigA[:, 5120:6144]]
    vt_b = [bigA[:, 6144:7168], bigA[:, 7168:8192]]
    sloc_b = [bigA[:, 8192:9216], bigA[:, 9216:10240]]
    yt_b = bigA[:, 10240:11264]
    am_b = bigA[:, 11264:13312]
    g2 = [bigA[:, 13312:14336], bigA[:, 14336:15360]]
    ygh = mxh
    ygl = mxl
    out_tok = [bigC[:, 1024 * i:1024 * i + 1024] for i in range(4)]
    sum_scr = bigC[:, 4096:4608]
    ffn_str = [bigC[:, 4608:5120], bigC[:, 5120:5632]]
    gf4 = bigC[:, 5632:6144]
    S.sem("Gout")
    t_par_mm = [0, 0]
    t_par_vv = [0, 0]
    t_pa_ev = [("vv", 0), ("vv", 0)]
    t_ffn_free = [("vv", 0), ("vv", 0)]
    tk = {}
    b_t = {"yg_used": [0, 0], "out_free": 0, "am_used": 0, "yt_ev": 0,
           "lloc": 0, "xpre": 0, "yg0": None}

    def load_chunk(cn, eng=SY):
        q = cn % 2
        pfx = "L" if eng is SY else "P"
        S.wait(eng, "mm", t_par_mm[q])
        S.wait(eng, "vv", t_par_vv[q])
        S.dma(eng,
              khat_bd[q][0:64, :].rearrange("i (p tt) -> i p tt", tt=256)[:, :, 0:128],
              spill["khat"].ap()[:, 0:64, L * cn:L * cn + L].rearrange("p i t -> i p t"),
              stream=pfx + "kh%d" % q)
        t_kh = S.dma(eng,
                     khat_bd[q][64:128, :].rearrange("i (p tt) -> i p tt", tt=256)[:, :, 128:256],
                     spill["khat"].ap()[:, 64:128, L * cn:L * cn + L].rearrange("p i t -> i p t"),
                     stream=pfx + "kh%d" % q)
        t_rt = S.dma(eng, rt_b[q].rearrange("i (p t) -> i p t", t=L),
                     spill["rt"].ap().rearrange("p i t -> i p t")[:, :, L * cn:L * cn + L],
                     stream=pfx + "rt%d" % q)
        t_vt = S.dma(eng, vt_b[q], s_vt.ap()[cn], stream=pfx + "vtb%d" % q)
        t_sl = S.dma(eng, sloc_b[q], s_sloc.ap()[cn], stream=pfx + "slb%d" % q)
        t_g = S.dma(eng, g2[q].rearrange("i (p t) -> i p t", t=L),
                    spill["g"].ap().rearrange("p i t -> i p t")[:, :, L * cn:L * cn + L],
                    stream=pfx + "g%d" % q)
        return {"kh": (pfx + "kh%d" % q, t_kh), "rt": (pfx + "rt%d" % q, t_rt),
                "vt": (pfx + "vtb%d" % q, t_vt), "sl": (pfx + "slb%d" % q, t_sl),
                "g": (pfx + "g%d" % q, t_g)}

    def b_prologue():
        S.wait(V, "ss", S.emitted["ss"])
        S.op(V, lambda e: e.memset(bigA[:, 0:4096], 0.0), inc="vv")
        S.op(V, lambda e: e.memset(s_inb[:, :], 0.0), inc="vv")
        S.drain(V)
        t_bms = S.emitted["vv"]
        t_par_vv[0] = t_bms
        t_par_vv[1] = t_bms
        tk[0] = load_chunk(0)
        tk[1] = load_chunk(1)
        b_t["xpre"] = S.dma(G, out_hbm[:, :], x_in[1:Tc + 1, :], stream="Gout")
        b_t["lloc"] = xf_done[0]
        b_t["yt_ev"] = xf_done[0]

    def b_chunk(c):
        st_, cc_ = c // 4, c % 4
        q = c % 2
        if c + 1 < NCH and c + 1 not in tk:
            tk[c + 1] = load_chunk(c + 1)
        t_kh, t_rt, t_vt, t_sl, t_g = (
            tk[c]["kh"], tk[c]["rt"], tk[c]["vt"], tk[c]["sl"], tk[c]["g"])
        S.wait(A_, "vv", b_t["lloc"])
        S.op(A_, lambda e: e.activation(plocc[:, :], cols[:, 152:160], AF.Exp, scale=-1.0), inc="ss")
        t_pl = S.emitted["ss"]
        S.wait(V, "ss", t_pl)
        S.wait(V, t_sl[0], t_sl[1])
        for p in range(NP):
            for hh in range(2):
                b0 = 64 * hh
                S.op(V, lambda e, p=p, b0=b0, q=q: e.scalar_tensor_tensor(
                    s_inb[b0:b0 + 64, 128 * p + b0:128 * p + b0 + 64],
                    s_pp[0][b0:b0 + 64, 128 * p + b0:128 * p + b0 + 64],
                    plocc[b0:b0 + 64, p:p + 1],
                    sloc_b[q][b0:b0 + 64, 128 * p + b0:128 * p + b0 + 64],
                    ALU.mult, ALU.add), inc="vv")
        t_sin = S.emitted["vv"]
        S.op(V, lambda e, c=c: e.tensor_tensor(
            cols[:, 152:160], cols[:, 152:160], cend_sb[:, c::NCH], ALU.add), inc="vv")
        S.drain(V)
        b_t["lloc"] = S.emitted["vv"]
        S.wait(TE, t_kh[0], t_kh[1])
        S.wait(TE, t_rt[0], t_rt[1])
        for grp in range(4):
            bank = grp % 2
            sem, val = t_pa_ev[bank]
            S.wait(TE, sem, val)
            for hh in range(4):
                h = 4 * grp + hh
                p = h // 2
                S.op(TE, lambda e, h=h, p=p, bank=bank, hh=hh, q=q: e.matmul(
                    p_a[bank][:, 128 * hh:128 * hh + 128],
                    khat_bd[q][:, 128 * h:128 * h + 128],
                    rt_b[q][:, 128 * p:128 * p + 128],
                    start=(hh == 0), stop=(hh == 3)), inc="mm")
            t_amm = S.emitted["mm"]
            S.wait(V, "mm", t_amm)
            S.wait(V, "mm", b_t["am_used"])
            S.op(V, lambda e, grp=grp, bank=bank: e.tensor_tensor(
                am_b[:, 512 * grp:512 * grp + 512], p_a[bank][:, :],
                maska[:, :], ALU.mult), inc="vv")
            t_pa_ev[bank] = ("vv", S.emitted["vv"])
        t_amask = S.emitted["vv"]
        S.wait(TE, "vv", t_amask)
        S.wait(TE, "vv", t_sin)
        S.wait(TE, t_vt[0], t_vt[1])
        S.wait(TE, "vv", b_t["yt_ev"])
        for b in range(2):
            for hh in range(8):
                h = 8 * b + hh
                S.op(TE, lambda e, h=h, b=b, hh=hh, q=q: e.matmul(
                    p_yt[:, 512 * b + 64 * hh: 512 * b + 64 * hh + 64],
                    am_b[:, 128 * h:128 * h + 128],
                    vt_b[q][:, 64 * h:64 * h + 64],
                    start=(hh == 0), stop=False), inc="mm")
            for pp_ in range(4):
                p = 4 * b + pp_
                S.op(TE, lambda e, p=p, b=b, pp_=pp_, q=q: e.matmul(
                    p_yt[:, 512 * b + 128 * pp_: 512 * b + 128 * pp_ + 128],
                    rt_b[q][:, 128 * p:128 * p + 128],
                    s_inb[:, 128 * p:128 * p + 128],
                    start=False, stop=(pp_ == 3)), inc="mm")
        t_ymm = S.emitted["mm"]
        b_t["am_used"] = t_ymm
        t_par_mm[q] = t_ymm
        S.wait(V, "mm", t_ymm)
        for h in range(H):
            S.op(V, lambda e, h=h, c=c, q=q: e.scalar_tensor_tensor(
                yt_b[:, 64 * h:64 * h + 64], vt_b[q][:, 64 * h:64 * h + 64],
                dvals_sb[:, c * H + h:c * H + h + 1],
                p_yt[:, 64 * h:64 * h + 64], ALU.mult, ALU.add), inc="vv")
        t_yev = S.emitted["vv"]
        b_t["yt_ev"] = t_yev
        S.wait(TE, "vv", t_yev)
        S.wait(V, t_g[0], t_g[1])
        if c == 0:
            S.wait(V, "Gff0", S.emitted["Gff0"])
            S.wait(V, "Gff1", S.emitted["Gff1"])
        ybuf = st_ % 2
        for grp in range(2):
            sem, val = pm_ev[grp]
            S.wait(TE, sem, val)
            for i4 in range(4):
                p = 4 * grp + i4
                S.op(TE, lambda e, p=p, g2_=grp, i4=i4: e.transpose(
                    p_m[g2_][:, 128 * i4:128 * i4 + 128],
                    yt_b[:, 128 * p:128 * p + 128], ident[:, :]), inc="mm")
            t_tp = S.emitted["mm"]
            S.wait(V, "mm", t_tp)
            gsl = g2[q][:, 512 * grp:512 * grp + 512]
            ysl = ygh[ybuf][:, :].rearrange("i (p t) -> i p t", t=TCH)[
                :, 4 * grp:4 * grp + 4, 128 * cc_:128 * cc_ + 128]
            ysll = ygl[ybuf][:, :].rearrange("i (p t) -> i p t", t=TCH)[
                :, 4 * grp:4 * grp + 4, 128 * cc_:128 * cc_ + 128]
            S.op(V, lambda e, g2_=grp, gsl=gsl: e.tensor_tensor(
                gf4, p_m[g2_][:, :], gsl, ALU.mult), inc="vv")
            S.drain(V)
            t_gf = S.emitted["vv"]
            pm_ev[grp] = ("vv", t_gf)
            S.wait(A_, "vv", t_gf)
            S.op(A_, lambda e, ysl=ysl: e.copy(
                ysl, gf4.rearrange("i (p t) -> i p t", t=128)), inc="ss")
            t_h = S.emitted["ss"]
            S.wait(V, "ss", t_h)
            S.op(V, lambda e, ysl=ysl, ysll=ysll: e.tensor_tensor(
                ysll, gf4.rearrange("i (p t) -> i p t", t=128), ysl,
                ALU.subtract), inc="vv")
            S.drain(V)
        t_par_vv[q] = S.emitted["vv"]


    t_rfs_used = [("vv", 0), ("vv", 0)]
    t_ffnw = [0, 0]
    S.sem("Gff0")
    S.sem("Gff1")
    xf_m = [0]

    def xf_next():
        if xf_m[0] >= NCORES:
            return
        if xf_m[0] == 0:
            xf_load(0)
        xf_fold(xf_m[0])
        xf_m[0] += 1

    for st in range(NST):
        t_mix = mix_kf[st] if st in mix_kf else emit_mix(st, 104, 136, xnv)
        if st >= 1:
            xf_next()
        kbuf = 1 - st % 2

        def evac_kf(dd, bank, t_mm, kbuf=kbuf):
            S.wait(V, "mm", t_mm)
            if dd == 0:
                S.wait(V, "mm", mx_used[kbuf])
                S.wait(A_, "mm", mx_used[kbuf])
            S.op(V, lambda e, b=bank: e.tensor_scalar(
                relscr, p_proj[b][:, :], 0.0, None, ALU.max, ALU.bypass), inc="vv")
            S.drain(V)
            S.op(V, lambda e: e.tensor_tensor(sqf, relscr, relscr, ALU.mult), inc="vv")
            S.drain(V)
            t_q = S.emitted["vv"]
            set_bank_free(bank, "vv", t_q)
            S.wait(A_, "vv", t_q)
            S.op(A_, lambda e, dd=dd, kbuf=kbuf: e.copy(
                mxh[kbuf][:, TCH * dd:TCH * dd + TCH], sqf), inc="ss")
            t_h = S.emitted["ss"]
            S.wait(V, "ss", t_h)
            S.op(V, lambda e, dd=dd, kbuf=kbuf: e.tensor_tensor(
                mxl[kbuf][:, TCH * dd:TCH * dd + TCH], sqf,
                mxh[kbuf][:, TCH * dd:TCH * dd + TCH], ALU.subtract), inc="vv")
            S.drain(V)
        emit_matmuls("fk", mxh[st % 2], mxl[st % 2], t_mix, evac_kf)
        mx_used[st % 2] = S.emitted["mm"]
        pump_weights()
        t_ksq = (S.emitted["ss"], S.emitted["vv"])
        if st >= 1:
            xf_next()
        if st == NST - 1:
            b_prologue()

        def evac_fv(dd, bank, t_mm):
            sem, val = t_rfs_used[dd % 2]
            S.wait(SY, sem, val)
            t_rfs = S.dma(SY, rfstr[dd % 2],
                          spill["rf"].ap()[dd][:, TCH * st:TCH * st + TCH],
                          stream="Lrf%d" % (dd % 2))
            S.wait(V, "mm", t_mm)
            S.wait(V, "Lrf%d" % (dd % 2), t_rfs)
            S.wait(V, "Gff%d" % (dd % 2), t_ffnw[dd % 2])
            S.op(V, lambda e, b=bank, dd=dd: e.tensor_tensor(
                fb[dd % 2], p_proj[b][:, :], rfstr[dd % 2], ALU.mult), inc="vv")
            S.drain(V)
            t_fb = S.emitted["vv"]
            t_rfs_used[dd % 2] = ("vv", t_fb)
            set_bank_free(bank, "vv", t_fb)
            S.wait(G, "vv", t_fb)
            t_ffnw[dd % 2] = S.dma(
                G, spill["ffn"].ap()[dd][:, TCH * st:TCH * st + TCH],
                fb[dd % 2], stream="Gff%d" % (dd % 2))
        emit_matmuls("fv", mxh[kbuf], mxl[kbuf], t_ksq, evac_fv)
        mx_used[kbuf] = S.emitted["mm"]
        pump_weights()
        if st >= 1:
            xf_next()
        if st == NST - 1:
            b_t["yg_used"][0] = S.emitted["mm"]
            S.wait(V, "mm", b_t["yg_used"][0])
            S.wait(A_, "mm", b_t["yg_used"][0])
            for _c in range(4):
                S.pump_bg(4)
                b_chunk(_c)
            b_t["yg0"] = (S.emitted["ss"], S.emitted["vv"])
    while xf_m[0] < NCORES:
        xf_next()
    finish_weight("fk")
    finish_weight("fv")
    S.barrier()

    if _stopped("S6"):
        return _finish(nc, S, ctx)

    if _stopped("XF"):
        return _finish(nc, S, ctx)
    for st in range(NST):
        if st == 0 and b_t["yg0"] is not None:
            t_yg = b_t["yg0"]
        else:
            S.wait(V, "mm", b_t["yg_used"][st % 2])
            S.wait(A_, "mm", b_t["yg_used"][st % 2])
            for cc_ in range(4):
                S.pump_bg(4)
                b_chunk(4 * st + cc_)
            t_yg = (S.emitted["ss"], S.emitted["vv"])
        if _stopped("B_tr"):
            return _finish(nc, S, ctx)

        def evac_wo(dd, bank, t_mm):
            sem, val = t_ffn_free[dd % 2]
            S.wait(SY, sem, val)
            t_ff = S.dma(SY, ffn_str[dd % 2],
                         spill["ffn"].ap()[dd][:, TCH * st:TCH * st + TCH],
                         stream="Lfn%d" % (dd % 2))
            S.wait(V, "mm", t_mm)
            S.wait(V, "Lfn%d" % (dd % 2), t_ff)
            S.op(V, lambda e, b=bank, dd=dd: e.scalar_tensor_tensor(
                sum_scr, p_proj[b][:, :], 1.0, ffn_str[dd % 2],
                ALU.mult, ALU.add), inc="vv")
            t_sum = S.emitted["vv"]
            t_ffn_free[dd % 2] = ("vv", t_sum)
            set_bank_free(bank, "vv", t_sum)
            S.wait(TE, "vv", t_sum)
            for tt in range(4):
                bk = tt % 2
                sem2, val2 = pm_ev[bk]
                S.wait(TE, sem2, val2)
                S.op(TE, lambda e, tt=tt, b=bk: e.transpose(
                    p_m[b][:, 128:256], sum_scr[:, 128 * tt:128 * tt + 128], ident[:, :]), inc="mm")
                t_tp = S.emitted["mm"]
                S.wait(A_, "mm", t_tp)
                if dd == 0:
                    S.wait(A_, "Gout", b_t["out_free"])
                S.op(A_, lambda e, tt=tt, dd=dd, b=bk: e.copy(
                    out_tok[tt][:, 128 * dd:128 * dd + 128],
                    p_m[b][:, 128:256]), inc="ss")
                pm_ev[bk] = ("ss", S.emitted["ss"])
        emit_matmuls("wo", ygh[st % 2], ygl[st % 2], t_yg, evac_wo)
        b_t["yg_used"][st % 2] = S.emitted["mm"]
        pump_weights()
        t_all = S.emitted["ss"]
        if _stopped("B_wo"):
            return _finish(nc, S, ctx)
        S.wait(G, "ss", t_all)
        S.wait(G, "Gout", b_t["xpre"])
        for tt in range(4):
            b_t["out_free"] = S.dma(G, out_hbm[TCH * st + 128 * tt: TCH * st + 128 * tt + 128, :],
                                    out_tok[tt], stream="Gout", accum=True)
    finish_weight("wo")
    S.wait(G, "Gout", S.emitted["Gout"])

    return _finish(nc, S, ctx)


def _finish(nc, S, ctx):
    for sname in list(S.emitted):
        S.handles[sname] = ctx.enter_context(nc.semaphore("sem_" + sname))
    with nc.Block() as block:
        @block.sync
        def _(eng):
            for f in S.q["sync"]:
                f(eng)

        @block.tensor
        def _(eng):
            for f in S.q["tensor"]:
                f(eng)

        @block.vector
        def _(eng):
            for f in S.q["vector"]:
                f(eng)

        @block.scalar
        def _(eng):
            for f in S.q["scalar"]:
                f(eng)

        @block.gpsimd
        def _(eng):
            for f in S.q["gpsimd"]:
                f(eng)

    ctx.close()
    return nc


def kernel(**inputs):
    x = np.asarray(inputs["L0_attn_x"], dtype=np.float32).reshape(T, D)
    wmap = {
        "w_ww": inputs["L0_attn_Ww"], "w_wk": inputs["L0_attn_Wk"],
        "w_wv": inputs["L0_attn_Wv"], "w_wr": inputs["L0_attn_Wr"],
        "w_wg": inputs["L0_attn_Wg"], "w_fr": inputs["L0_ffn_W_r"],
        "w_fk": inputs["L0_ffn_W_k"], "w_fv": inputs["L0_ffn_W_v"],
        "w_wo": inputs["L0_attn_Wo"],
    }
    wmap = {k: np.ascontiguousarray(np.asarray(v, dtype=np.float32).T)
            for k, v in wmap.items()}
    vecs = np.ascontiguousarray(np.stack([
        np.asarray(inputs["L0_attn_mix_k"], np.float32).ravel(),
        np.asarray(inputs["L0_attn_mix_v"], np.float32).ravel(),
        np.asarray(inputs["L0_attn_mix_r"], np.float32).ravel(),
        np.asarray(inputs["L0_attn_mix_w"], np.float32).ravel(),
        np.asarray(inputs["L0_attn_time_decay"], np.float32).ravel(),
        np.asarray(inputs["L0_attn_time_first"], np.float32).ravel(),
        np.asarray(inputs["L0_attn_ln_w"], np.float32).ravel(),
        np.asarray(inputs["L0_attn_ln_b"], np.float32).ravel(),
        np.asarray(inputs["L0_ffn_mu_r"], np.float32).ravel(),
        np.asarray(inputs["L0_ffn_mu_k"], np.float32).ravel(),
    ]))
    ident = np.eye(P, dtype=np.float32)
    maska = np.ascontiguousarray(
        np.tile(np.triu(np.ones((P, P), np.float32), 1), (1, 4)))
    onesb = np.zeros((P, 2), np.float32)
    onesb[:64, 0] = 1.0
    onesb[64:, 1] = 1.0
    epsc = np.full((P, 1), EPS, np.float32)

    in_maps = []
    for ci in range(NCORES):
        lead = np.zeros((1, D), np.float32) if ci == 0 else x[Tc * ci - 1: Tc * ci]
        xs = np.ascontiguousarray(np.concatenate([lead, x[Tc * ci: Tc * (ci + 1)]], 0))
        cmask = np.zeros((P, 8), np.float32)
        cmask[:, :ci] = 1.0
        pmask = np.full((P, 1), 0.0 if ci == 0 else 1.0, np.float32)
        m = {"x": xs, "vecs": vecs, "ident": ident, "maska": maska,
             "onesb": onesb, "cmask": cmask, "pmask": pmask, "epsc": epsc}
        m.update(wmap)
        in_maps.append(m)

    nc = build_program()
    res = run_bass_kernel_spmd(nc, in_maps, core_ids=list(range(NCORES)))
    out = np.concatenate([res.results[ci]["out"] for ci in range(NCORES)], axis=0)
    return np.ascontiguousarray(out.reshape(1, T, D).astype(np.float32))
